# version 01
# baseline (speedup 1.0000x reference)
import math
from contextlib import ExitStack

import numpy as np
import concourse.bass as bass
import concourse.mybir as mybir
from concourse.bass_utils import run_bass_kernel_spmd

F32 = mybir.dt.float32
BF16 = mybir.dt.bfloat16
AF = mybir.ActivationFunctionType
ALU = mybir.AluOpType
AX = mybir.AxisListType

ENGS = ("tensor", "vector", "scalar", "gpsimd", "sync")
DMA_QUEUES = ("sync", "gpsimd", "scalar")

T = 2048
C = 1024
NSEQ = 2
NT = T // 128
NC = C // 128



def _ap_region(ap):
    t = ap.tensor
    name = t.name
    space = type(t).__name__
    dims = [list(d) for d in ap.ap]
    if "SB" in space or "PSum" in space:
        rowlen = dims[0][0] if len(dims) > 1 else 1
    else:
        try:
            rowlen = int(np.prod(t.shape[1:])) if len(t.shape) > 1 else 1
        except Exception:
            rowlen = 1
    try:
        p0 = ap.base_partition()
        pn = ap.partition_size()
    except Exception:
        p0, pn = 0, 128
    off = ap.offset
    if not isinstance(off, int):
        return (name, 0, 128, 0, 1 << 60)
    f0 = off - p0 * rowlen
    span = 0
    for step, count in dims[1:]:
        if step >= rowlen and rowlen > 1:
            return (name, 0, 128, 0, 1 << 60)
        span += abs(step) * (count - 1)
    if f0 < 0:
        return (name, 0, 128, 0, 1 << 60)
    esz = 4 if ap.dtype == F32 else 2
    f0 *= esz
    f1 = f0 + (span + 1) * esz
    if "PSum" in space:
        f0 = (f0 // 2048) * 2048
        f1 = ((f1 + 2047) // 2048) * 2048
    return (name, p0, p0 + pn, f0, f1)


def _overlap(a, b):
    return (a[0] == b[0] and a[2] > b[1] and b[2] > a[1]
            and a[4] > b[3] and b[4] > a[3])


def _covers(old, new):
    return (old[1] >= new[1] and old[2] <= new[2]
            and old[3] >= new[3] and old[4] <= new[4])


class Node:
    __slots__ = ("eng", "fn", "args", "kwargs", "is_dma", "gidx", "deps",
                 "marked", "incval", "qidx", "is_mm")

    def __init__(self, eng, fn, args, kwargs, is_dma, gidx, is_mm):
        self.eng, self.fn, self.args, self.kwargs = eng, fn, args, kwargs
        self.is_dma, self.gidx, self.is_mm = is_dma, gidx, is_mm
        self.deps = []
        self.marked = False
        self.incval = None
        self.qidx = None


DMA_SEM_POOL = 20


class Prog:
    def __init__(self, nc):
        self.nc = nc
        self.nodes = []
        self.qcount = {q: 0 for q in DMA_QUEUES}
        self.log = {}
        self.stack = ExitStack()
        self._namecnt = {}
        self._last_pe = None

    def _uname(self, base):
        n = self._namecnt.get(base, 0)
        self._namecnt[base] = n + 1
        return base if n == 0 else f"{base}_{n}"

    def sb(self, name, p, f, dtype=F32):
        return self.stack.enter_context(
            self.nc.sbuf_tensor(self._uname(name), [p, f], dtype))

    def psum(self, name, p, f, dtype=F32):
        return self.stack.enter_context(
            self.nc.psum_tensor(self._uname(name), [p, f], dtype))

    def _record(self, eng, fn, args, kwargs, outs, ins, is_dma=False, is_mm=False,
                extra_dep=None):
        reads = [_ap_region(a) for a in ins if a is not None]
        writes = [_ap_region(a) for a in outs if a is not None]
        node = Node(eng, fn, args, kwargs, is_dma, len(self.nodes), is_mm)
        if is_dma:
            node.qidx = self.qcount[eng]
            self.qcount[eng] += 1
        deps = {}

        def add_dep(m):
            if m is node:
                return
            if (node.eng == "tensor" and node.is_mm and not node.is_dma
                    and m.eng == "tensor" and m.is_mm and not m.is_dma):
                return
            deps[m.gidx] = m

        for r in reads:
            lg = self.log.get(r[0])
            if lg:
                for (m, mr) in lg["w"]:
                    if _overlap(r, mr):
                        add_dep(m)
        for w in writes:
            lg = self.log.get(w[0])
            if lg:
                for (m, mr) in lg["w"]:
                    if _overlap(w, mr):
                        add_dep(m)
                for (m, mr) in lg["r"]:
                    if _overlap(w, mr):
                        add_dep(m)
        if extra_dep is not None:
            deps[extra_dep.gidx] = extra_dep
        if eng == "tensor" and is_mm:
            tmode = (fn == "transpose")
            if self._last_pe is not None and self._last_pe[1] != tmode:
                deps[self._last_pe[0].gidx] = self._last_pe[0]
            self._last_pe = (node, tmode)
        node.deps = [self.nodes[g] for g in sorted(deps)]
        self.nodes.append(node)
        for r in reads:
            self.log.setdefault(r[0], {"w": [], "r": []})["r"].append((node, r))
        for w in writes:
            lg = self.log.setdefault(w[0], {"w": [], "r": []})
            lg["w"] = [e for e in lg["w"] if not _covers(e[1], w)]
            lg["r"] = [e for e in lg["r"] if not _covers(e[1], w)]
            lg["w"].append((node, w))
        return node

    def mm(self, out, lhsT, rhs, start=True, stop=True, tile_position=None):
        kw = dict(start=start, stop=stop)
        if tile_position is not None:
            kw["tile_position"] = tile_position
        return self._record("tensor", "matmul", (out, lhsT, rhs), kw,
                            [out], [lhsT, rhs], is_mm=True)

    def transpose(self, out, in_, ident):
        return self._record("tensor", "transpose", (out, in_, ident), {},
                            [out], [in_, ident], is_mm=True)

    def act(self, out, in_, func, bias=None, scale=None, accum_out=None):
        kw = {}
        ins = [in_]
        if bias is not None:
            kw["bias"] = bias
            if isinstance(bias, bass.AP):
                ins.append(bias)
        if scale is not None:
            kw["scale"] = scale
            if isinstance(scale, bass.AP):
                ins.append(scale)
        outs = [out]
        if accum_out is not None:
            kw["accum_out"] = accum_out
            outs.append(accum_out)
        return self._record("scalar", "activation", (out, in_, func), kw, outs, ins)

    def tt(self, out, a, b, alu, eng="vector"):
        return self._record(eng, "tensor_tensor", (out, a, b), dict(op=alu),
                            [out], [a, b])

    def ts(self, out, in0, s1, s2, op0, op1=None, accum_out=None, eng="vector",
           extra_dep=None):
        args = (out, in0, s1, s2)
        kw = dict(op0=op0)
        if op1 is not None:
            kw["op1"] = op1
        ins = [in0] + [s for s in (s1, s2) if isinstance(s, bass.AP)]
        outs = [out]
        if accum_out is not None:
            kw["accum_out"] = accum_out
            outs.append(accum_out)
        return self._record(eng, "tensor_scalar", args, kw, outs, ins,
                            extra_dep=extra_dep)

    def stt(self, out, in0, s, in1, op0, op1, eng="vector"):
        ins = [in0, in1] + ([s] if isinstance(s, bass.AP) else [])
        return self._record(eng, "scalar_tensor_tensor", (out, in0, s, in1),
                            dict(op0=op0, op1=op1), [out], ins)

    def red(self, out, in_, alu, negate=False, eng="vector"):
        kw = dict(axis=AX.X, op=alu)
        if negate:
            kw["negate"] = True
        return self._record(eng, "tensor_reduce", (out, in_), kw, [out], [in_])

    def copy(self, out, in_, eng="vector"):
        fn = "copy" if eng == "scalar" else "tensor_copy"
        return self._record(eng, fn, (out, in_), {}, [out], [in_])

    def memset(self, ap, v, eng="vector"):
        return self._record(eng, "memset", (ap, v), {}, [ap], [])

    def recip(self, out, in_):
        return self._record("vector", "reciprocal", (out, in_), {}, [out], [in_])

    def select(self, out, mask, a, b):
        return self._record("vector", "select", (out, mask, a, b), {},
                            [out], [mask, a, b])

    def dma(self, q, out, in_, **kw):
        return self._record(q, "dma_start", (), dict(out=out, in_=in_, **kw),
                            [out], [in_], is_dma=True)

    def build(self):
        for n in self.nodes:
            for m in n.deps:
                if not m.is_dma:
                    m.marked = True
        cnt = {e: 0 for e in ENGS}
        for n in self.nodes:
            if not n.is_dma and n.marked:
                cnt[n.eng] += 1
                n.incval = cnt[n.eng]
        nc = self.nc
        with ExitStack() as ss:
            esem = {e: ss.enter_context(nc.semaphore(f"se_{e}")) for e in ENGS}
            qsem = {q: [ss.enter_context(nc.semaphore(f"sq_{q}{i}"))
                        for i in range(DMA_SEM_POOL)] for q in DMA_QUEUES}

            def dma_sem_val(m):
                return (qsem[m.eng][m.qidx % DMA_SEM_POOL],
                        16 * (m.qidx // DMA_SEM_POOL + 1))

            with nc.Block() as block:
                for eng in ENGS:
                    mynodes = [n for n in self.nodes if n.eng == eng]
                    if not mynodes:
                        continue

                    def emit(handle, mynodes=mynodes):
                        waited = {}
                        for n in mynodes:
                            for m in n.deps:
                                if m.is_dma:
                                    sem, val = dma_sem_val(m)
                                else:
                                    sem, val = esem[m.eng], m.incval
                                k = id(sem)
                                if waited.get(k, 0) < val:
                                    handle.wait_ge(sem, val)
                                    waited[k] = val
                            if n.is_dma:
                                sem, val = dma_sem_val(n)
                                if n.qidx >= DMA_SEM_POOL:
                                    gv = val - 16
                                    k = id(sem)
                                    if waited.get(k, 0) < gv:
                                        handle.wait_ge(sem, gv)
                                        waited[k] = gv
                            inst = getattr(handle, n.fn)(*n.args, **n.kwargs)
                            if n.is_dma:
                                sem, val = dma_sem_val(n)
                                inst.then_inc(sem, 16)
                            elif n.marked:
                                inst.then_inc(esem[n.eng], 1)

                    getattr(block, eng)(emit)

                def drain(handle):
                    for q in DMA_QUEUES:
                        nq = self.qcount[q]
                        for s in range(min(nq, DMA_SEM_POOL)):
                            uses = (nq - s + DMA_SEM_POOL - 1) // DMA_SEM_POOL
                            handle.wait_ge(qsem[q][s], 16 * uses)
                block.vector(drain)
        return self



def make_constants():
    k = np.arange(128)[:, None]
    u = np.arange(1024)[None, :]
    return {
        "c_ident": np.eye(128, dtype=np.float32),
        "c_band": ((u - k) >= 384).astype(np.float32)[:, :1024],
        "c_ones": np.ones((1, 128), dtype=np.float32),
        "c_one1": np.ones((1, 1), dtype=np.float32),
        "c_m1": (np.arange(1024)[None, :] // 16 == np.arange(64)[:, None])
        .astype(np.float32),
        "c_m2": (np.arange(120)[None, :] - 56 == np.arange(128)[:, None] // 16)
        .astype(np.float32),
        "c_rope": _rope_const(),
        "c_lmask": _lmask_const(),
        "c_drow": (-(2.0 ** -20) * np.arange(512, dtype=np.float32)).reshape(1, 512),
        "c_e1": _e_const(0),
        "c_e2": _e_const(1),
        "c_dcol": (-(2.0 ** -20) * np.arange(128, dtype=np.float32)).reshape(128, 1),
    }


def _e_const(which):
    e = np.zeros((64, 256), dtype=np.float32)
    for a in range(16):
        for b in range(16):
            d = a if which == 0 else b
            e[d, a * 16 + b] = 1.0
    return e


def _rope_const():
    t = np.arange(T, dtype=np.float64)
    inv = 1.0 / 10000.0 ** (np.arange(0, 32, 2, dtype=np.float64) / 32.0)
    freqs = t[:, None] * inv[None, :]
    emb = np.concatenate([freqs, freqs], -1)
    scale = (np.cos(emb) + np.sin(emb)).astype(np.float32)
    out = np.zeros((128, T), dtype=np.float32)
    for g in range(2):
        out[g * 64:g * 64 + 32, :] = scale.T
    return out


def _lmask_const():
    k = np.arange(128)[:, None]
    q = np.arange(128)[None, :]
    mA = ((k >= 64) & (q < 64)).astype(np.float32)
    mB = ((k < 64) & (q >= 64)).astype(np.float32)
    return np.concatenate([mA, mB], axis=1)


_DECL_CACHE = {}


def declare_in(nc, name, shape, dtype=F32):
    key = (id(nc), name)
    if key not in _DECL_CACHE:
        _DECL_CACHE[key] = nc.dram_tensor(name, list(shape), dtype,
                                          kind="ExternalInput").ap()
    return _DECL_CACHE[key]


def declare_out(nc, name, shape, dtype=F32):
    return nc.dram_tensor(name, list(shape), dtype, kind="ExternalOutput").ap()


class Com:

    def __init__(self, p):
        nc = p.nc
        self.p = p
        self.ident = p.sb("ident", 128, 128, BF16)
        self.band = p.sb("band", 128, 896, BF16)
        self.ones_row = p.sb("ones_row", 1, 128, BF16)
        self.ones_rowf = p.sb("ones_rowf", 1, 128, F32)
        self.one1f = p.sb("one1f", 1, 1, F32)
        d_id = declare_in(nc, "c_ident", (128, 128))
        d_band = declare_in(nc, "c_band", (128, 1024))
        d_ones = declare_in(nc, "c_ones", (1, 128))
        d_one1 = declare_in(nc, "c_one1", (1, 1))
        p.dma("gpsimd", self.ident[:, :], d_id)
        p.dma("gpsimd", self.band[:, :], d_band[:, 0:896])
        p.dma("gpsimd", self.ones_row[:, :], d_ones)
        p.dma("sync", self.ones_rowf[:, :], d_ones)
        p.dma("sync", self.one1f[:, :], d_one1)
        self.AR = [p.sb(f"AR{i}", 128, 2048, BF16) for i in range(16)]
        self.AR8 = [p.sb(f"AH{i}", 128, 2048, BF16) for i in range(8)]
        self.XN = [p.sb(f"XNf{i}", 128, 2048, BF16) for i in range(NC)]
        self.GA = [p.sb(f"GAc{i}", 128, 1024, BF16) for i in range(NT)]
        self.stat = p.sb("stat", 128, 128, F32)
        self.small = p.sb("small", 128, 32, F32)
        self.xrot = [p.sb(f"xrot{i}", 128, 1024, F32) for i in range(1)]
        self.xnrot = [p.sb(f"xnrot{i}", 128, 1024, BF16) for i in range(1)]
        self.scr = [p.sb(f"scr{i}", 128, 1024, F32) for i in range(2)]
        self.pp = [p.sb(f"pp{i}", 128, 512, BF16) for i in range(2)]
        self.vv = [p.sb(f"vvt{i}", 128, 80, BF16) for i in range(2)]
        self.rowbuf = p.sb("rowbuf", 1, 512, F32)
        self.wstr = [p.sb(f"wstr{i}", 128, 1024, BF16) for i in range(2)]
        self.wcol = [p.sb(f"wcol{i}", 128, 128, BF16) for i in range(2)]
        self.wg8 = p.sb("wg8", 128, 64, BF16)
        self.gates = p.sb("gates", 128, 128, F32)
        self.bc1 = p.sb("bc1", 128, 1024, F32)
        self.bc2 = p.sb("bc2", 128, 1024, F32)
        self.bc3 = p.sb("bc3", 128, 1024, F32)
        self.brow = p.sb("brow", 1, 512, F32)
        self.b2sb = p.sb("b2sb", 8, 1024, BF16)
        self.rrow = p.sb("rrow", 1, 512, BF16)
        self.bcol = p.sb("bcol", 128, 24, F32)
        self.ceps = p.sb("ceps", 128, 1, F32)
        p.memset(self.ceps[:, :], 1e-5)
        self.m2sb = p.sb("m2sb", 128, 120, BF16)
        p.dma("gpsimd", self.m2sb[:, :], declare_in(nc, "c_m2", (128, 120)))
        self.lmask = p.sb("lmask", 128, 256, BF16)
        p.dma("gpsimd", self.lmask[:, :], declare_in(nc, "c_lmask", (128, 256)))
        self.dcol = p.sb("dcol", 128, 1, F32)
        p.dma("sync", self.dcol[:, :], declare_in(nc, "c_dcol", (128, 1)))
        self.identf32 = p.sb("identf32", 128, 128, F32)
        p.dma("sync", self.identf32[:, :], declare_in(nc, "c_ident", (128, 128)))
        self.psA = [p.psum(f"psA{i}", 128, 512, F32) for i in range(4)]
        self.psO = [p.psum(f"psO{i}", 128, 512, F32) for i in range(2)]
        self.psT = [p.psum(f"psT{i}", 128, 128, BF16) for i in range(2)]

    def row_to_bc(self, row_ap, dst, n, ps=None):
        p = self.p
        ps = ps if ps is not None else self.psA[3]
        ones = self.ones_rowf if row_ap.dtype == F32 else self.ones_row
        p.mm(ps[:, 0:n], ones[0:1, :], row_ap, start=True, stop=True)
        p.copy(dst, ps[:, 0:n], eng="scalar")

    def row_to_col(self, row_ap, dst_col, n, ps=None):
        p = self.p
        ps = ps if ps is not None else self.psA[3]
        p.mm(ps[0:n, 0:1], row_ap, self.one1f[0:1, 0:1], start=True, stop=True)
        p.copy(dst_col, ps[0:n, 0:1], eng="vector")



def emit_ln_stage(p, com, x_h, seq, eps=1e-5):
    stat = com.stat
    for i in range(NT):
        xr = com.xrot[0]
        p.dma("sync", xr[:, :], x_h[seq, i * 128:(i + 1) * 128, :])
        m = stat[:, 8 * i:8 * i + 1]
        r = stat[:, 8 * i + 1:8 * i + 2]
        s0 = stat[:, 8 * i + 2:8 * i + 3]
        s1 = stat[:, 8 * i + 3:8 * i + 4]
        v = stat[:, 8 * i + 4:8 * i + 5]
        t = stat[:, 8 * i + 5:8 * i + 6]
        p.red(s0, xr[:, :], ALU.add)
        p.act(com.scr[i % 2][:, :], xr[:, :], AF.Square, accum_out=s1)
        p.ts(m, s0, 1.0 / C, None, ALU.mult)
        p.ts(v, s1, 1.0 / C, None, ALU.mult)
        p.ts(t, m, m, None, ALU.mult)
        p.tt(v, v, t, ALU.subtract)
        p.act(t, v, AF.Ln, bias=com.ceps[:, 0:1])
        p.act(r, t, AF.Exp, scale=-0.5)
        xn = com.xnrot[0]
        p.ts(xn[:, :], xr[:, :], m, r, ALU.subtract, ALU.mult)
        for j in range(NC):
            ps = com.psT[j % 2]
            p.transpose(ps[:, :], xn[:, j * 128:(j + 1) * 128], com.ident[:, :])
            p.copy(com.XN[j][:, i * 128:(i + 1) * 128], ps[:, :],
                   eng="vector" if j % 2 == 0 else "scalar")


def emit_final_update(p, com, x_h, y_h, seq, i, s_f32, gbuf=None):
    g = gbuf if gbuf is not None else com.scr[1][:, :]
    p.act(g, com.GA[i][:, :], AF.Silu)
    xr = com.xrot[0]
    p.dma("sync", xr[:, :], x_h[seq, i * 128:(i + 1) * 128, :])
    u = g
    p.tt(u, s_f32, g, ALU.mult)
    p.tt(u, u, xr[:, :], ALU.add)
    p.dma("sync", y_h[seq, i * 128:(i + 1) * 128, :], u)



def emit_layer2(p, com, io, seq):
    nc = p.nc
    x_h, y_h = io["x"], io["y"]
    wqkv_h = io["L2_attn_w_qkv"]
    wproj_h = io["L2_attn_w_proj"]
    bproj_h = io["L2_attn_b_proj"]
    lng_h = io["L2_attn_ln_gamma"]
    lnb_h = io["L2_attn_ln_beta"]
    wg_h = io["L2_ffn_Wg"]
    bg_h = io["L2_ffn_bg"]
    w1_h = io["L2_ffn_W1"]
    b1_h = io["L2_ffn_b1"]
    w2_h = io["L2_ffn_W2"]
    b2_h = io["L2_ffn_b2"]

    psA, psO, psT = com.psA, com.psO, com.psT
    XN = com.XN

    for src_h, dst in ((bproj_h, com.bc1), (lng_h, com.bc2), (lnb_h, com.bc3)):
        for ni in range(2):
            p.dma("gpsimd", com.rrow[0:1, :],
                  src_h.rearrange("(o c) -> o c", o=1)[0:1, ni * 512:(ni + 1) * 512])
            com.row_to_bc(com.rrow[0:1, :], dst[:, ni * 512:(ni + 1) * 512], 512)
    p.dma("gpsimd", com.b2sb[:, :], b2_h[:, :])
    wg_sb = com.wg8
    p.dma("gpsimd", wg_sb[:, 0:64].rearrange("p (c e) -> p c e", e=8),
          wg_h.rearrange("(c p) e -> p c e", p=128))
    p.dma("gpsimd", com.rrow[0:1, 0:8], bg_h.rearrange("(o c) -> o c", o=1))
    bg_bc = com.bcol[:, 8:16]
    com.row_to_bc(com.rrow[0:1, 0:8], bg_bc, 8)

    emit_ln_stage(p, com, x_h, seq)

    for i in range(NT):
        ps = psA[i % 2]
        for kc in range(NC):
            p.mm(ps[:, 0:8], XN[kc][:, i * 128:(i + 1) * 128],
                 wg_sb[:, kc * 8:(kc + 1) * 8],
                 start=(kc == 0), stop=(kc == NC - 1))
        lg = com.scr[0][:, 0:8]
        p.tt(lg, ps[:, 0:8], bg_bc, ALU.add)
        mx = com.small[:, 0:1]
        p.red(mx, lg, ALU.max, negate=True)
        ex = com.scr[0][:, 8:16]
        ssum = com.small[:, 1:2]
        p.act(ex, lg, AF.Exp, bias=mx, accum_out=ssum)
        rs_ = com.small[:, 2:3]
        p.recip(rs_, ssum)
        mxe = com.small[:, 3:4]
        p.red(mxe, ex, ALU.max)
        msk = com.scr[0][:, 16:24]
        p.ts(msk, ex, mxe, None, ALU.is_ge)
        w_ = com.small[:, 4:5]
        p.ts(w_, mxe, rs_, None, ALU.mult)
        p.ts(com.gates[:, 8 * i:8 * i + 8], msk, w_, None, ALU.mult)

    gtb = com.wcol[0]
    for i in range(NT):
        p.copy(gtb[:, 0:8], com.gates[:, 8 * i:8 * i + 8])
        p.transpose(psT[0][:, :], gtb[:, 0:128], com.ident[:, :])
        gT = com.wcol[0]
        p.copy(gT[:, :], psT[0][:, :])
        for half in range(2):
            p.mm(psO[half][:, :], gT[0:8, 0:128],
                 com.b2sb[:, half * 512:(half + 1) * 512], start=True, stop=True)
            p.copy(com.GA[i][:, half * 512:(half + 1) * 512], psO[half][:, :])

    for e in range(8):
        for fh in range(2):
            for fc in range(16):
                f0 = fh * 2048 + fc * 128
                if fc % 4 == 0:
                    p.dma("gpsimd", com.brow[0:1, 0:512],
                          b1_h[e, f0:f0 + 512].rearrange("(o f) -> o f", o=1))
                wb1 = com.wstr[fc % 2]
                p.dma("gpsimd", wb1[:, :].rearrange("p (c f) -> p c f", f=128),
                      w1_h[e].rearrange("(c p) f -> p c f", p=128)[:, :, f0:f0 + 128])
                p.dma("gpsimd", com.AR8[fc // 2][:, (fc % 2) * 1024:(fc % 2 + 1) * 1024],
                      w2_h[e, f0:f0 + 128, :])
                bcol = com.bcol[:, fc % 2:fc % 2 + 1]
                com.row_to_col(com.brow[0:1, (fc % 4) * 128:(fc % 4 + 1) * 128],
                               bcol, 128)
                hb = com.AR[fc]
                for ni in range(4):
                    ps = psA[ni % 2]
                    for kc in range(NC):
                        p.mm(ps[:, :], wb1[:, kc * 128:(kc + 1) * 128],
                             XN[kc][:, ni * 512:(ni + 1) * 512],
                             start=(kc == 0), stop=(kc == NC - 1))
                    p.act(hb[:, ni * 512:(ni + 1) * 512], ps[:, :], AF.Silu,
                          bias=bcol)
            for i in range(NT):
                for half in range(2):
                    ps = psO[half]
                    for fc in range(16):
                        p.mm(ps[:, :], com.AR[fc][:, i * 128:(i + 1) * 128],
                             com.AR8[fc // 2][:, (fc % 2) * 1024 + half * 512:
                                              (fc % 2) * 1024 + (half + 1) * 512],
                             start=(fc == 0), stop=(fc == 15))
                    gcol = com.gates[:, 8 * i + e:8 * i + e + 1]
                    sl = slice(half * 512, (half + 1) * 512)
                    p.stt(com.GA[i][:, sl], ps[:, :], gcol, com.GA[i][:, sl],
                          ALU.mult, ALU.add)

    SC = 0.125
    for kc in range(NC):
        p.dma("gpsimd", com.AR8[kc // 2][:, (kc % 2) * 1024:(kc % 2 + 1) * 1024],
              wqkv_h[kc * 128:(kc + 1) * 128, 2 * C:3 * C])
    for i in range(NT):
        for half in range(2):
            ps = psA[half]
            for kc in range(NC):
                p.mm(ps[:, :], XN[kc][:, i * 128:(i + 1) * 128],
                     com.AR8[kc // 2][:, (kc % 2) * 1024 + half * 512:
                                      (kc % 2) * 1024 + (half + 1) * 512],
                     start=(kc == 0), stop=(kc == NC - 1))
            p.copy(com.AR[8 + i // 2][:, (i % 2) * 1024 + half * 512:
                                      (i % 2) * 1024 + (half + 1) * 512], ps[:, :])

    def v_ap(kt, h):
        return com.AR[8 + kt // 2][:, (kt % 2) * 1024 + h * 64:(kt % 2) * 1024 + (h + 1) * 64]

    for hg in range(2):
        for qk in range(2):
            for mi in range(4):
                mg = hg * 4 + mi
                for c in range(NC):
                    wc = com.wcol[c % 2]
                    p.dma("gpsimd", wc[:, :],
                          wqkv_h[c * 128:(c + 1) * 128,
                                 qk * C + mg * 128:qk * C + (mg + 1) * 128])
                    for ni in range(4):
                        p.mm(psA[ni % 4][:, :], wc[:, :],
                             XN[c][:, ni * 512:(ni + 1) * 512],
                             start=(c == 0), stop=(c == NC - 1))
                for ni in range(4):
                    p.copy(com.AR[qk * 4 + mi][:, ni * 512:(ni + 1) * 512],
                           psA[ni % 4][:, :])
        for hh in range(8):
            h = hg * 8 + hh
            hc, hr = divmod(hh, 2)
            qs_feat = com.AR[hc]
            ks_feat = com.AR[4 + hc]
            for qs in range(4):
                out_ps = psO[qs % 2]
                nkt = 4 * (qs + 1)
                for kt in range(nkt):
                    sc_ps = psA[kt % 2]
                    p.mm(sc_ps[:, :],
                         ks_feat[hr * 64:hr * 64 + 64, kt * 128:(kt + 1) * 128],
                         qs_feat[hr * 64:hr * 64 + 64, qs * 512:(qs + 1) * 512],
                         start=True, stop=True)
                    acol = com.small[:, 8 + kt % 2:9 + kt % 2]
                    p.red(acol, sc_ps[:, :], ALU.max, negate=True)
                    sb_a = com.small[:, 10 + kt % 2:11 + kt % 2]
                    p.ts(sb_a, acol, SC, None, ALU.mult)
                    ppt = com.pp[kt % 2]
                    p.act(ppt[:, :], sc_ps[:, :], AF.Exp, bias=sb_a, scale=SC)
                    qlo, klo = qs * 512, kt * 128
                    if klo + 127 >= qlo:
                        off = klo - qlo
                        p.tt(ppt[:, :], ppt[:, :],
                             com.band[:, 384 - off:896 - off], ALU.mult)
                    expa = com.small[:, 12 + kt % 2:13 + kt % 2]
                    p.act(expa, acol, AF.Exp, scale=-SC)
                    vt = com.vv[kt % 2]
                    p.ts(vt[:, 0:64], v_ap(kt, h), expa, None, ALU.mult)
                    p.copy(vt[:, 64:65], expa)
                    p.mm(out_ps[0:65, :], vt[:, 0:65], ppt[:, :],
                         start=(kt == 0), stop=(kt == nkt - 1))
                rz = com.rowbuf[0:1, 0:512]
                p.recip(rz, out_ps[64:65, 0:512])
                rb = com.scr[1][(qs % 2) * 64:(qs % 2) * 64 + 64, 0:512]
                p.mm(psA[2][0:64, 0:512], com.ones_rowf[0:1, 0:64], rz,
                     start=True, stop=True)
                p.copy(rb, psA[2][0:64, 0:512], eng="scalar")
                p.tt(com.AR8[h // 2][hr * 64:hr * 64 + 64, qs * 512:(qs + 1) * 512],
                     out_ps[0:64, :], rb, ALU.mult)

    for kc in range(NC):
        p.dma("gpsimd", com.AR[kc // 2][:, (kc % 2) * 1024:(kc % 2 + 1) * 1024],
              wproj_h[kc * 128:(kc + 1) * 128, :])
    for i in range(NT):
        for half in range(2):
            ps = psO[half]
            for kc in range(NC):
                p.mm(ps[:, :], com.AR8[kc][:, i * 128:(i + 1) * 128],
                     com.AR[kc // 2][:, (kc % 2) * 1024 + half * 512:
                                     (kc % 2) * 1024 + (half + 1) * 512],
                     start=(kc == 0), stop=(kc == NC - 1))
        t1 = com.scr[0]
        xr = com.xrot[0]
        p.dma("sync", xr[:, :], x_h[seq, i * 128:(i + 1) * 128, :])
        m = com.stat[:, 8 * i:8 * i + 1]
        r = com.stat[:, 8 * i + 1:8 * i + 2]
        p.ts(t1[:, :], xr[:, :], m, r, ALU.subtract, ALU.mult)
        t2 = com.scr[1]
        for half in range(2):
            sl = slice(half * 512, (half + 1) * 512)
            p.tt(t2[:, sl], psO[half][:, :], com.bc1[:, sl], ALU.add)
        p.tt(t1[:, :], t1[:, :], t2[:, :], ALU.add)
        s0 = com.small[:, 16:17]
        s1 = com.small[:, 17:18]
        p.red(s0, t1[:, :], ALU.add)
        p.act(t2[:, :], t1[:, :], AF.Square, accum_out=s1)
        m2 = com.small[:, 18:19]
        r2 = com.small[:, 19:20]
        v2 = com.small[:, 20:21]
        tq = com.small[:, 21:22]
        p.ts(m2, s0, 1.0 / C, None, ALU.mult)
        p.ts(v2, s1, 1.0 / C, None, ALU.mult)
        p.ts(tq, m2, m2, None, ALU.mult)
        p.tt(v2, v2, tq, ALU.subtract)
        p.act(tq, v2, AF.Ln, bias=com.ceps[:, 0:1])
        p.act(r2, tq, AF.Exp, scale=-0.5)
        p.ts(t2[:, :], t1[:, :], m2, r2, ALU.subtract, ALU.mult)
        p.tt(t2[:, :], t2[:, :], com.bc2[:, :], ALU.mult)
        p.tt(t1[:, :], t2[:, :], com.bc3[:, :], ALU.add)
        emit_final_update(p, com, x_h, y_h, seq, i, t1[:, :])



L2_INPUT_NAMES = ["L2_attn_w_qkv", "L2_attn_w_proj", "L2_attn_b_proj",
                  "L2_attn_ln_gamma", "L2_attn_ln_beta", "L2_ffn_Wg",
                  "L2_ffn_bg", "L2_ffn_W1", "L2_ffn_b1", "L2_ffn_W2",
                  "L2_ffn_b2"]


def build_layer2_program():
    nc = bass.Bass("TRN2", target_bir_lowering=False, debug=False)
    p = Prog(nc)
    com = Com(p)
    io = {"x": declare_in(nc, "x", (NSEQ, T, C)),
          "y": declare_out(nc, "y", (NSEQ, T, C))}
    shapes = {"L2_attn_w_qkv": (C, 3 * C), "L2_attn_w_proj": (C, C),
              "L2_attn_b_proj": (C,), "L2_attn_ln_gamma": (C,),
              "L2_attn_ln_beta": (C,), "L2_ffn_Wg": (C, 8), "L2_ffn_bg": (8,),
              "L2_ffn_W1": (8, C, 4096), "L2_ffn_b1": (8, 4096),
              "L2_ffn_W2": (8, 4096, C), "L2_ffn_b2": (8, C)}
    for n in L2_INPUT_NAMES:
        io[n] = declare_in(nc, n, shapes[n])
    for seq in range(NSEQ):
        emit_layer2(p, com, io, seq)
    p.build()
    return nc, p


def run_layer2(x, weights):
    nc, p = build_layer2_program()
    cst = make_constants()
    in_maps = []
    for core in range(8):
        m = {"x": np.ascontiguousarray(x[core * 2:(core + 1) * 2])}
        m.update({k: np.asarray(v, dtype=np.float32) for k, v in weights.items()})
        m.update(cst)
        in_maps.append(m)
    res = run_bass_kernel_spmd(nc, in_maps, core_ids=list(range(8)))
    out = np.concatenate([r["y"] for r in res.results], axis=0)
    p.stack.close()
    return out


def kernel(**inputs):
    x = np.ascontiguousarray(np.asarray(inputs["L0_attn_hidden_states"],
                                        dtype=np.float32))
    x = run_layer0(x, {k: inputs[k] for k in L0_INPUT_NAMES})
    x = run_layer1(x, {k: inputs[k] for k in L1_INPUT_NAMES})
    x = run_layer2(x, {k: inputs[k] for k in L2_INPUT_NAMES})
    x = run_layer3(x, {k: inputs[k] for k in L3_INPUT_NAMES})
    return x



def emit_layer1(p, com, io, seq):
    nc = p.nc
    x_h, y_h = io["x"], io["y"]
    wq1_h = io["L1_attn_wq1"]
    wk1_h = io["L1_attn_wk1"]
    wq2_h = io["L1_attn_wq2"]
    wk2_h = io["L1_attn_wk2"]
    wv_h = io["L1_attn_wv"]
    lng_h = io["L1_attn_ln_gamma"]
    lnb_h = io["L1_attn_ln_beta"]
    lamb_h = io["L1_attn_lamb"]
    win_h = io["L1_ffn_Win"]
    bin_h = io["L1_ffn_bin_"]
    w1_h = io["L1_ffn_W1"]
    b1_h = io["L1_ffn_b1"]
    w2_h = io["L1_ffn_W2"]
    b2_h = io["L1_ffn_b2"]
    wout_h = io["L1_ffn_Wout"]
    bout_h = io["L1_ffn_bout"]

    psA, psO, psT = com.psA, com.psO, com.psT
    XN = com.XN
    AR, AR8 = com.AR, com.AR8

    for src_h, dst in ((bout_h, com.bc1), (lng_h, com.bc2), (lnb_h, com.bc3)):
        for ni in range(2):
            p.dma("gpsimd", com.rrow[0:1, :],
                  src_h.rearrange("(o c) -> o c", o=1)[0:1, ni * 512:(ni + 1) * 512])
            com.row_to_bc(com.rrow[0:1, :], dst[:, ni * 512:(ni + 1) * 512], 512)
    lrow = com.rowbuf[0:1, 0:1]
    p.dma("sync", lrow, lamb_h.rearrange("(o c) -> o c", o=1))
    p.mm(psA[3][:, 0:1], com.ones_rowf[0:1, :], lrow, start=True, stop=True)
    lneg = com.bcol[:, 16:17]
    loneminus = com.bcol[:, 17:18]
    p.ts(lneg, psA[3][:, 0:1], -1.0, None, ALU.mult)
    p.ts(loneminus, psA[3][:, 0:1], -1.0, 1.0, ALU.mult, ALU.add)
    p.dma("sync", com.rowbuf[0:1, 0:64], bin_h.rearrange("(o c) -> o c", o=1))
    bincol = com.bcol[:, 18:19]
    com.row_to_col(com.rowbuf[0:1, 0:64], bincol[0:64, 0:1], 64)
    p.dma("sync", com.rowbuf[0:1, 0:64], b2_h.rearrange("(o c) -> o c", o=1))
    b2col = com.bcol[:, 19:20]
    com.row_to_col(com.rowbuf[0:1, 0:64], b2col[0:64, 0:1], 64)

    emit_ln_stage(p, com, x_h, seq)

    s_feat = AR8[6]
    for kc in range(NC):
        wc = com.wcol[kc % 2]
        p.dma("gpsimd", wc[:, 0:64], win_h[kc * 128:(kc + 1) * 128, :])
        for ni in range(4):
            p.mm(psA[ni][0:64, :], wc[:, 0:64],
                 XN[kc][:, ni * 512:(ni + 1) * 512],
                 start=(kc == 0), stop=(kc == NC - 1))
    for ni in range(4):
        p.ts(s_feat[0:64, ni * 512:(ni + 1) * 512], psA[ni][0:64, :],
             bincol[0:64, 0:1], None, ALU.add)
    w1sb = com.wg8
    p.dma("gpsimd", w1sb[0:64, 0:16], w1_h[:, :])
    m1sb = AR8[5]
    p.dma("gpsimd", m1sb[0:64, 0:1024], declare_in(nc, "c_m1", (64, 1024)))
    ew1 = AR8[7]
    w1b = w1sb[0:64, 0:16].unsqueeze(1).broadcast_to([64, 64, 16])
    p.tt(ew1[0:64, 0:1024].rearrange("p (a b) -> p a b", b=16), w1b,
         m1sb[0:64, 0:1024].rearrange("p (a b) -> p a b", b=16), ALU.mult)
    b1_flat = b1_h.rearrange("a b -> (a b)").rearrange("(o c) -> o c", o=1)
    p.dma("sync", com.rowbuf[0:1, 0:512], b1_flat[0:1, 0:512])
    p.dma("sync", com.brow[0:1, 0:512], b1_flat[0:1, 512:1024])
    for mi in range(NC):
        b1c = com.bcol[:, 20 + mi % 2:21 + mi % 2]
        srcrow = com.rowbuf[0:1, (mi % 4) * 128:(mi % 4 + 1) * 128] if mi < 4 else \
            com.brow[0:1, (mi % 4) * 128:(mi % 4 + 1) * 128]
        com.row_to_col(srcrow, b1c, 128, ps=psO[0])
        for ni in range(4):
            ps = psA[ni % 2]
            p.mm(ps[:, :], ew1[0:64, mi * 128:(mi + 1) * 128],
                 s_feat[0:64, ni * 512:(ni + 1) * 512], start=True, stop=True)
            p.act(AR[mi][:, ni * 512:(ni + 1) * 512], ps[:, :], AF.Relu, bias=b1c)
    w2_flat = w2_h.rearrange("a b -> (a b)").rearrange("(o c) -> o c", o=1)
    p.dma("sync", com.rowbuf[0:1, 0:512], w2_flat[0:1, 0:512])
    p.dma("sync", com.brow[0:1, 0:512], w2_flat[0:1, 512:1024])
    g2 = AR8[7]
    for mi in range(NC):
        w2c = com.bcol[:, 20 + mi % 2:21 + mi % 2]
        srcrow = com.rowbuf[0:1, (mi % 4) * 128:(mi % 4 + 1) * 128] if mi < 4 else \
            com.brow[0:1, (mi % 4) * 128:(mi % 4 + 1) * 128]
        com.row_to_col(srcrow, w2c, 128, ps=psO[0])
        ew2 = com.wcol[mi % 2]
        w2bc = w2c.broadcast_to([128, 64])
        p.tt(ew2[:, 0:64], w2bc, com.m2sb[:, 56 - 8 * mi:120 - 8 * mi], ALU.mult)
        for ni in range(4):
            p.mm(psA[ni][0:64, :], ew2[:, 0:64],
                 AR[mi][:, ni * 512:(ni + 1) * 512],
                 start=(mi == 0), stop=(mi == NC - 1))
    for ni in range(4):
        p.ts(g2[0:64, ni * 512:(ni + 1) * 512], psA[ni][0:64, :],
             b2col[0:64, 0:1], None, ALU.add)
    wout_sb = AR8[5]
    p.dma("gpsimd", wout_sb[0:64, 0:1024], wout_h[:, :])
    for i in range(NT):
        for half in range(2):
            ps = psA[half]
            p.mm(ps[:, :], g2[0:64, i * 128:(i + 1) * 128],
                 wout_sb[0:64, half * 512:(half + 1) * 512], start=True, stop=True)
        for half in range(2):
            sl = slice(half * 512, (half + 1) * 512)
            p.tt(com.GA[i][:, sl], psA[half][:, :], com.bc1[:, sl], ALU.add)

    SC = 0.125

    def proj_set(w_h, dst_tiles, hg):
        for mi in range(4):
            for kc in range(NC):
                stg = com.scr[kc % 2]
                h0 = hg * 8 + 2 * mi
                p.dma("sync", stg[:, 0:64], w_h[h0, kc * 128:(kc + 1) * 128, :])
                p.dma("sync", stg[:, 64:128], w_h[h0 + 1, kc * 128:(kc + 1) * 128, :])
                wc = com.wcol[kc % 2]
                p.copy(wc[:, :], stg[:, 0:128])
                for ni in range(4):
                    p.mm(psA[ni][:, :], wc[:, :], XN[kc][:, ni * 512:(ni + 1) * 512],
                         start=(kc == 0), stop=(kc == NC - 1))
            for ni in range(4):
                p.copy(dst_tiles[mi][:, ni * 512:(ni + 1) * 512], psA[ni][:, :])

    for hg in range(2):
        q1t, k1t = AR[0:4], AR[4:8]
        q2t, k2t = AR[8:12], AR[12:16]
        proj_set(wq1_h, q1t, hg)
        proj_set(wk1_h, k1t, hg)
        proj_set(wq2_h, q2t, hg)
        proj_set(wk2_h, k2t, hg)
        scrb = [com.scr[0][:, :].bitcast(BF16), com.scr[1][:, :].bitcast(BF16)]
        for kc in range(NC):
            dst = scrb[kc // 4][:, (kc % 4) * 512:(kc % 4 + 1) * 512]
            p.dma("gpsimd", dst.rearrange("p (h d) -> p h d", d=64),
                  wv_h[hg * 8:(hg + 1) * 8, kc * 128:(kc + 1) * 128, :]
                  .rearrange("h c d -> c h d"))
        for i in range(NT):
            ps = psA[i % 2]
            for kc in range(NC):
                p.mm(ps[:, :], XN[kc][:, i * 128:(i + 1) * 128],
                     scrb[kc // 4][:, (kc % 4) * 512:(kc % 4 + 1) * 512],
                     start=(kc == 0), stop=(kc == NC - 1))
            p.copy(AR8[4 * hg + i // 4][:, (i % 4) * 512:(i % 4 + 1) * 512], ps[:, :])

        def v_ap(kt, hh, hg=hg):
            return AR8[4 * hg + kt // 4][
                :, (kt % 4) * 512 + hh * 64:(kt % 4) * 512 + (hh + 1) * 64]

        ydst = AR[0:4]
        for hh in range(8):
            hc, hr = divmod(hh, 2)
            for qs in range(4):
                for kt in range(NT):
                    sc1 = psA[kt % 2]
                    sc2 = psA[2 + kt % 2]
                    p.mm(sc1[:, :], k1t[hc][hr * 64:hr * 64 + 64, kt * 128:(kt + 1) * 128],
                         q1t[hc][hr * 64:hr * 64 + 64, qs * 512:(qs + 1) * 512],
                         start=True, stop=True)
                    p.mm(sc2[:, :], k2t[hc][hr * 64:hr * 64 + 64, kt * 128:(kt + 1) * 128],
                         q2t[hc][hr * 64:hr * 64 + 64, qs * 512:(qs + 1) * 512],
                         start=True, stop=True)
                    a1 = com.small[:, 6 + kt % 2:7 + kt % 2]
                    a2 = com.small[:, 8 + kt % 2:9 + kt % 2]
                    p.red(a1, sc1[:, :], ALU.max, negate=True)
                    p.red(a2, sc2[:, :], ALU.max, negate=True)
                    ac = com.small[:, 10 + kt % 2:11 + kt % 2]
                    p.tt(ac, a1, a2, ALU.min)
                    sb_a = com.small[:, 12 + kt % 2:13 + kt % 2]
                    p.ts(sb_a, ac, SC, None, ALU.mult)
                    pp1 = com.pp[kt % 2]
                    p.act(pp1[:, :], sc1[:, :], AF.Exp, bias=sb_a, scale=SC)
                    expa = com.small[:, 14 + kt % 2:15 + kt % 2]
                    p.act(expa, ac, AF.Exp, scale=-SC)
                    vt = com.vv[kt % 2]
                    p.ts(vt[:, 0:64], v_ap(kt, hh), expa, None, ALU.mult)
                    p.copy(vt[:, 64:65], expa)
                    p.mm(psO[0][0:65, :], vt[:, 0:65], pp1[:, :],
                         start=(kt == 0), stop=(kt == NT - 1))
                    p.act(pp1[:, :], sc2[:, :], AF.Exp, bias=sb_a, scale=SC)
                    p.mm(psO[1][0:65, :], vt[:, 0:65], pp1[:, :],
                         start=(kt == 0), stop=(kt == NT - 1))
                rz = com.rowbuf[0:1, 0:512]
                rb1 = com.scr[1][64:128, 0:512]
                rb2 = com.scr[1][64:128, 512:1024]
                p.recip(rz, psO[0][64:65, 0:512])
                p.mm(psA[2][64:128, 0:512], com.ones_rowf[0:1, 0:64], rz,
                     start=True, stop=True)
                p.copy(rb1, psA[2][64:128, 0:512], eng="scalar")
                p.recip(rz, psO[1][64:65, 0:512])
                p.mm(psA[3][64:128, 0:512], com.ones_rowf[0:1, 0:64], rz,
                     start=True, stop=True)
                p.copy(rb2, psA[3][64:128, 0:512], eng="scalar")
                ta = com.scr[0][64:128, 0:512]
                tb = com.scr[0][64:128, 512:1024]
                p.tt(ta, psO[0][0:64, :], rb1, ALU.mult)
                p.tt(tb, psO[1][0:64, :], rb2, ALU.mult)
                p.stt(ydst[hc][hr * 64:hr * 64 + 64, qs * 512:(qs + 1) * 512],
                      tb, com.bcol[64:128, 16:17], ta, ALU.mult, ALU.add)

        for j in range(4):
            srct = ydst[j]
            for i in range(NT):
                ps = com.psT[i % 2]
                p.transpose(ps[:, :], srct[:, i * 128:(i + 1) * 128], com.ident[:, :])
                p.copy(AR8[4 * hg + i // 4][:, (i % 4) * 512 + j * 128:
                                            (i % 4) * 512 + (j + 1) * 128],
                       ps[:, :], eng="vector" if i % 2 == 0 else "scalar")

    if io.get("dbg_y") is not None and seq == 0:
        for i in range(NT):
            ylo_ = AR8[i // 4][:, (i % 4) * 512:(i % 4) * 512 + 512]
            yhi_ = AR8[4 + i // 4][:, (i % 4) * 512:(i % 4) * 512 + 512]
            p.dma("gpsimd", io["dbg_y"][i * 128:(i + 1) * 128, 0:512], ylo_)
            p.dma("gpsimd", io["dbg_y"][i * 128:(i + 1) * 128, 512:1024], yhi_)
    for i in range(NT):
        ylo = AR8[i // 4][:, (i % 4) * 512:(i % 4) * 512 + 512]
        yhi = AR8[4 + i // 4][:, (i % 4) * 512:(i % 4) * 512 + 512]
        s0 = com.small[:, 16:17]
        s0b = com.small[:, 22:23]
        s1 = com.small[:, 17:18]
        s1b = com.small[:, 23:24]
        t2 = com.scr[1]
        p.red(s0, ylo, ALU.add)
        p.red(s0b, yhi, ALU.add)
        p.act(t2[:, 0:512], ylo, AF.Square, accum_out=s1)
        p.act(t2[:, 512:1024], yhi, AF.Square, accum_out=s1b)
        p.tt(s0, s0, s0b, ALU.add)
        p.tt(s1, s1, s1b, ALU.add)
        m2 = com.small[:, 18:19]
        r2 = com.small[:, 19:20]
        v2 = com.small[:, 20:21]
        tq = com.small[:, 21:22]
        p.ts(m2, s0, 1.0 / C, None, ALU.mult)
        p.ts(v2, s1, 1.0 / C, None, ALU.mult)
        p.ts(tq, m2, m2, None, ALU.mult)
        p.tt(v2, v2, tq, ALU.subtract)
        p.act(tq, v2, AF.Ln, bias=com.ceps[:, 0:1])
        p.act(r2, tq, AF.Exp, scale=-0.5)
        t1 = com.scr[0]
        p.ts(t2[:, 0:512], ylo, m2, r2, ALU.subtract, ALU.mult)
        p.ts(t2[:, 512:1024], yhi, m2, r2, ALU.subtract, ALU.mult)
        p.tt(t2[:, :], t2[:, :], com.bc2[:, :], ALU.mult)
        p.tt(t1[:, :], t2[:, :], com.bc3[:, :], ALU.add)
        p.ts(t1[:, :], t1[:, :], com.bcol[:, 17:18], None, ALU.mult)
        emit_final_update(p, com, x_h, y_h, seq, i, t1[:, :])


L1_INPUT_NAMES = ["L1_attn_wq1", "L1_attn_wk1", "L1_attn_wq2", "L1_attn_wk2",
                  "L1_attn_wv", "L1_attn_ln_gamma", "L1_attn_ln_beta",
                  "L1_attn_lamb", "L1_ffn_Win", "L1_ffn_bin_", "L1_ffn_W1",
                  "L1_ffn_b1", "L1_ffn_W2", "L1_ffn_b2", "L1_ffn_Wout",
                  "L1_ffn_bout"]

L1_SHAPES = {"L1_attn_wq1": (16, C, 64), "L1_attn_wk1": (16, C, 64),
             "L1_attn_wq2": (16, C, 64), "L1_attn_wk2": (16, C, 64),
             "L1_attn_wv": (16, C, 64), "L1_attn_ln_gamma": (C,),
             "L1_attn_ln_beta": (C,), "L1_attn_lamb": (1,),
             "L1_ffn_Win": (C, 64), "L1_ffn_bin_": (64,),
             "L1_ffn_W1": (64, 16), "L1_ffn_b1": (64, 16),
             "L1_ffn_W2": (64, 16), "L1_ffn_b2": (64,),
             "L1_ffn_Wout": (64, C), "L1_ffn_bout": (C,)}


def build_layer1_program():
    global _DECL_CACHE
    _DECL_CACHE = {}
    nc = bass.Bass("TRN2", target_bir_lowering=False, debug=False)
    p = Prog(nc)
    com = Com(p)
    io = {"x": nc.dram_tensor("x", [NSEQ, T, C], F32, kind="ExternalInput").ap(),
          "y": declare_out(nc, "y", (NSEQ, T, C))}
    import os
    if os.environ.get("L1_DEBUG"):
        io["dbg_y"] = declare_out(nc, "dbg_y", (T, C))
    for n in L1_INPUT_NAMES:
        io[n] = declare_in(nc, n, L1_SHAPES[n])
    for seq in range(NSEQ):
        emit_layer1(p, com, io, seq)
    p.build()
    return nc, p


def run_layer1(x, weights, want_dbg=False):
    nc, p = build_layer1_program()
    cst = make_constants()
    in_maps = []
    for core in range(8):
        m = {"x": np.ascontiguousarray(x[core * 2:(core + 1) * 2])}
        for k, v in weights.items():
            a = np.asarray(v, dtype=np.float32)
            if k == "L1_attn_lamb":
                a = a.reshape(1)
            m[k] = a
        m.update(cst)
        in_maps.append(m)
    res = run_bass_kernel_spmd(nc, in_maps, core_ids=list(range(8)))
    out = np.concatenate([r["y"] for r in res.results], axis=0)
    p.stack.close()
    if want_dbg:
        return out, res.results[0].get("dbg_y")
    return out



def emit_l3_ffn(p, com, io, seq):
    nc = p.nc
    psA, psO = com.psA, com.psO
    XN = com.XN
    AR, AR8 = com.AR, com.AR8
    DELTA = 2.0 ** -20
    H, DH = 8, 64
    SCL = DH ** -0.5
    scratch = io["l3scr"]
    scrb = [com.scr[0][:, :].bitcast(BF16), com.scr[1][:, :].bitcast(BF16)]

    for name_h, coff in ((io["L3_ffn_ln_g"], 0), (io["L3_ffn_ln_b"], 8),
                         (io["L3_ffn_rms_g"], 16)):
        for kc in range(NC):
            p.dma("sync", com.rowbuf[0:1, 0:128],
                  name_h.rearrange("(o c) -> o c", o=1)[0:1, kc * 128:(kc + 1) * 128])
            com.row_to_col(com.rowbuf[0:1, 0:128],
                           com.gates[:, coff + kc:coff + kc + 1], 128)
    for ni in range(2):
        p.dma("gpsimd", com.rrow[0:1, :],
              io["L3_ffn_null_q"].rearrange("(o c) -> o c", o=1)[0:1, ni * 512:(ni + 1) * 512])
        com.row_to_bc(com.rrow[0:1, :], com.bc1[:, ni * 512:(ni + 1) * 512], 512)

    for kc in range(NC):
        p.dma("sync", com.rowbuf[0:1, 0:128],
              io["L3_ffn_rt_q"][0:1, kc * 128:(kc + 1) * 128])
        com.row_to_col(com.rowbuf[0:1, 0:128],
                       com.gates[:, 48 + kc:48 + kc + 1], 128)
    rtcol_b = com.wg8[:, 33:41]
    for kc in range(NC):
        p.copy(rtcol_b[:, kc:kc + 1], com.gates[:, 48 + kc:48 + kc + 1])
    for n in range(4):
        ps = psA[n]
        for kc in range(NC):
            p.mm(ps[0:1, :], rtcol_b[:, kc:kc + 1],
                 XN[kc][:, n * 512:(n + 1) * 512],
                 start=(kc == 0), stop=(kc == NC - 1))
        p.copy(com.rowbuf[0:1, 0:512], ps[0:1, :])
        p.dma("sync", scratch[n * 512:(n + 1) * 512].rearrange("(o c) -> o c", o=1),
              com.rowbuf[0:1, 0:512])
    s_r = com.bc3[0:32, 0:64]
    p.dma("sync", s_r, scratch[0:2048].rearrange("(a b) -> a b", a=32))
    b_r = com.bc3[0:32, 64:128]
    t_r = com.bc3[0:32, 128:192]
    e_r = com.bc3[0:32, 192:256]
    cols = com.bc3[0:32, 256:264]
    ones32f = com.bcol[:, 22:23]
    p.memset(ones32f[0:32, 0:1], 1.0)
    p.ts(b_r, s_r, -1.0, None, ALU.mult)
    LOGK = math.log(1152.0)
    a_sc = com.rowbuf[0:1, 200:201]
    for it in range(50):
        p.tt(t_r, s_r, b_r, ALU.add)
        mcol = cols[:, 0:1]
        p.red(mcol, t_r, ALU.max)
        p.mm(psO[0][0:1, 0:32], mcol, com.identf32[0:32, 0:32], start=True, stop=True)
        mrow = com.rowbuf[0:1, 204:236]
        p.copy(mrow, psO[0][0:1, 0:32])
        M = com.rowbuf[0:1, 240:241]
        p.red(M, mrow, ALU.max)
        p.mm(psO[1][0:32, 0:1], com.ones_rowf[0:1, 0:32], M, start=True, stop=True)
        negM = cols[:, 1:2]
        p.ts(negM[0:32, 0:1] if False else cols[0:32, 1:2], psO[1][0:32, 0:1],
             -1.0, None, ALU.mult)
        rsum = cols[:, 2:3]
        p.act(e_r, t_r, AF.Exp, bias=cols[0:32, 1:2], accum_out=cols[0:32, 2:3])
        p.mm(psO[0][0:1, 0:1], cols[0:32, 2:3], ones32f[0:32, 0:1],
             start=True, stop=True)
        lnz = com.rowbuf[0:1, 242:243]
        p.act(lnz, psO[0][0:1, 0:1], AF.Ln)
        p.tt(lnz, lnz, M, ALU.add)
        p.ts(a_sc, lnz, -1.0, LOGK, ALU.mult, ALU.add)
        p.mm(psO[1][0:32, 0:1], com.ones_rowf[0:1, 0:32], a_sc, start=True, stop=True)
        p.ts(t_r, s_r, psO[1][0:32, 0:1], None, ALU.add)
        p.ts(b_r, t_r, 0.0, -1.0, ALU.max, ALU.mult)
    u_r = com.bc3[0:32, 0:64]
    p.ts(u_r, t_r, 0.0, None, ALU.min)
    p.dma("sync", scratch[2048:4096].rearrange("(a b) -> a b", a=32), u_r)
    for i in range(NT):
        p.dma("sync", com.gates[:, 48 + 0:48 + 1] if False else
              com.gates[:, 48 + (i % 8):48 + (i % 8) + 1],
              scratch[2048 + i * 128:2048 + (i + 1) * 128]
              .rearrange("(a b) -> a b", b=1))
        kc_ = com.gates[:, 56 + (i % 8):56 + (i % 8) + 1] if False else None
        p.ts(com.gates[:, 64 + i:64 + i + 1], com.gates[:, 48 + (i % 8):48 + (i % 8) + 1],
             1.0, -DELTA * 128 * i, ALU.mult, ALU.add)
        p.tt(com.gates[:, 64 + i:64 + i + 1], com.gates[:, 64 + i:64 + i + 1],
             com.dcol[:, 0:1], ALU.add)
    p.dma("sync", com.brow[0:1, 0:512], declare_in(nc, "c_drow", (1, 512)))
    for n in range(4):
        p.dma("sync", com.rowbuf[0:1, 0:512],
              scratch[2048 + n * 512:2048 + (n + 1) * 512].rearrange("(o c) -> o c", o=1))
        p.ts(com.rowbuf[0:1, 0:512], com.rowbuf[0:1, 0:512], 1.0,
             -DELTA * 512 * n, ALU.mult, ALU.add)
        p.tt(com.rowbuf[0:1, 0:512], com.rowbuf[0:1, 0:512], com.brow[0:1, 0:512],
             ALU.add)
        com.row_to_bc(com.rowbuf[0:1, 0:512],
                      com.scr[n // 2][:, (n % 2) * 512:(n % 2 + 1) * 512], 512)
    for c in range(NT):
        for n in range(4):
            g_sl = com.pp[n % 2][:, :]
            p.ts(g_sl, com.scr[n // 2][:, (n % 2) * 512:(n % 2 + 1) * 512],
                 com.gates[:, 64 + c:64 + c + 1], None, ALU.is_lt)
            p.mm(psA[n][0:1, :], com.wg8[:, 32:33], g_sl,
                 start=(c == 0), stop=(c == NT - 1))
    for n in range(4):
        p.copy(com.rowbuf[0:1, 0:512], psA[n][0:1, :])
        p.dma("sync", scratch[n * 512:(n + 1) * 512].rearrange("(o c) -> o c", o=1),
              com.rowbuf[0:1, 0:512])
    for i in range(NT):
        p.dma("sync", com.gates[:, 64 + i:64 + i + 1],
              scratch[i * 128:(i + 1) * 128].rearrange("(a b) -> a b", b=1))
        p.ts(com.gates[:, 32 + i:32 + i + 1], com.gates[:, 64 + i:64 + i + 1],
             1023.5, None, ALU.is_lt)

    for kc in range(NC):
        p.ts(AR[kc][:, :], XN[kc][:, :], com.gates[:, kc:kc + 1],
             com.gates[:, 8 + kc:8 + kc + 1], ALU.mult, ALU.add)

    for qk in range(2):
        for mi in range(4):
            for kc in range(NC):
                wc = com.wcol[kc % 2]
                p.dma("gpsimd", wc[:, :],
                      io["L3_ffn_w_qkv_l"][kc * 128:(kc + 1) * 128,
                                           qk * 512 + mi * 128:qk * 512 + (mi + 1) * 128])
                for ni in range(4):
                    p.mm(psA[ni][:, :], wc[:, :], AR[kc][:, ni * 512:(ni + 1) * 512],
                         start=(kc == 0), stop=(kc == NC - 1))
            for ni in range(4):
                p.copy(AR[8 + 4 * qk + mi][:, ni * 512:(ni + 1) * 512], psA[ni][:, :])
    for kc in range(NC):
        p.dma("gpsimd", scrb[kc // 4][:, (kc % 4) * 512:(kc % 4 + 1) * 512],
              io["L3_ffn_w_qkv_l"][kc * 128:(kc + 1) * 128, 1024:1536])
    for i in range(NT):
        ps = psA[i % 2]
        for kc in range(NC):
            p.mm(ps[:, :], AR[kc][:, i * 128:(i + 1) * 128],
                 scrb[kc // 4][:, (kc % 4) * 512:(kc % 4 + 1) * 512],
                 start=(kc == 0), stop=(kc == NC - 1))
        p.copy(AR8[i // 4][:, (i % 4) * 512:(i % 4 + 1) * 512], ps[:, :])

    def vl_ap(kt, hh):
        return AR8[kt // 4][:, (kt % 4) * 512 + hh * 64:(kt % 4) * 512 + (hh + 1) * 64]

    for hh in range(H):
        hc, hr = divmod(hh, 2)
        qlf = AR[8 + hc]
        klf = AR[12 + hc]
        for qt in range(NT):
            kts = [kt for kt in (qt - 1, qt, qt + 1) if 0 <= kt < NT]
            out_ps = psO[qt % 2]
            for j, kt in enumerate(kts):
                sc_ps = psA[j % 2]
                p.mm(sc_ps[:, 0:128],
                     klf[hr * 64:hr * 64 + 64, kt * 128:(kt + 1) * 128],
                     qlf[hr * 64:hr * 64 + 64, qt * 128:(qt + 1) * 128],
                     start=True, stop=True)
                acol = com.small[:, 6 + j % 2:7 + j % 2]
                p.red(acol, sc_ps[:, 0:128], ALU.max, negate=True)
                sb_a = com.small[:, 8 + j % 2:9 + j % 2]
                p.ts(sb_a, acol, SCL, None, ALU.mult)
                ppt = com.pp[j % 2]
                p.act(ppt[:, 0:128], sc_ps[:, 0:128], AF.Exp, bias=sb_a, scale=SCL)
                if kt != qt:
                    moff = 0 if kt < qt else 128
                    p.tt(ppt[:, 0:128], ppt[:, 0:128],
                         com.lmask[:, moff:moff + 128], ALU.mult)
                expa = com.small[:, 10 + j % 2:11 + j % 2]
                p.act(expa, acol, AF.Exp, scale=-SCL)
                vt = com.vv[j % 2]
                p.ts(vt[:, 0:64], vl_ap(kt, hh), expa, None, ALU.mult)
                p.copy(vt[:, 64:65], expa)
                p.mm(out_ps[0:65, 0:128], vt[:, 0:65], ppt[:, 0:128],
                     start=(j == 0), stop=(j == len(kts) - 1))
            rz = com.rowbuf[0:1, 0:128]
            p.recip(rz, out_ps[64:65, 0:128])
            p.mm(psA[2][hr * 64:hr * 64 + 64, 0:128], com.ones_rowf[0:1, 0:64], rz,
                 start=True, stop=True)
            rbs = com.bc2[hr * 64:hr * 64 + 64, 0:128]
            p.copy(rbs, psA[2][hr * 64:hr * 64 + 64, 0:128], eng="scalar")
            p.tt(AR8[4 + hc][hr * 64:hr * 64 + 64, qt * 128:(qt + 1) * 128],
                 out_ps[0:64, 0:128], rbs, ALU.mult)

    for kc in range(4):
        p.dma("gpsimd", AR[kc // 2][:, (kc % 2) * 1024:(kc % 2 + 1) * 1024],
              io["L3_ffn_w_out_l"][kc * 128:(kc + 1) * 128, :])
    for i in range(NT):
        for half in range(2):
            ps = psO[half]
            for kc in range(4):
                p.mm(ps[:, :], AR8[4 + kc][:, i * 128:(i + 1) * 128],
                     AR[kc // 2][:, (kc % 2) * 1024 + half * 512:
                                 (kc % 2) * 1024 + (half + 1) * 512],
                     start=(kc == 0), stop=(kc == 3))
            p.copy(com.GA[i][:, half * 512:(half + 1) * 512], psO[half][:, :])

    for kc in range(NC):
        p.ts(AR[kc][:, :], XN[kc][:, :], com.gates[:, 16 + kc:16 + kc + 1],
             None, ALU.mult)
    for mi in range(4):
        for kc in range(NC):
            wc = com.wcol[kc % 2]
            p.dma("gpsimd", wc[:, :],
                  io["L3_ffn_w_q_h"][kc * 128:(kc + 1) * 128, mi * 128:(mi + 1) * 128])
            for ni in range(4):
                p.mm(psA[ni][:, :], wc[:, :], AR[kc][:, ni * 512:(ni + 1) * 512],
                     start=(kc == 0), stop=(kc == NC - 1))
        for ni in range(4):
            p.copy(AR[8 + mi][:, ni * 512:(ni + 1) * 512], psA[ni][:, :])
    for mi in range(4):
        for kc in range(NC):
            wc = com.wcol[kc % 2]
            p.dma("gpsimd", wc[:, :].rearrange("p (h d) -> p h d", d=64),
                  io["L3_ffn_w_kv_h"][kc * 128:(kc + 1) * 128, 2 * mi:2 * mi + 2, 0:64])
            for ni in range(4):
                p.mm(psA[ni][:, :], wc[:, :], AR[kc][:, ni * 512:(ni + 1) * 512],
                     start=(kc == 0), stop=(kc == NC - 1))
        for ni in range(4):
            p.copy(AR[12 + mi][:, ni * 512:(ni + 1) * 512], psA[ni][:, :])
    for kc in range(NC):
        p.dma("gpsimd",
              scrb[kc // 4][:, (kc % 4) * 512:(kc % 4 + 1) * 512]
              .rearrange("p (h d) -> p h d", d=64),
              io["L3_ffn_w_kv_h"][kc * 128:(kc + 1) * 128, :, 64:128])
    for i in range(NT):
        ps = psA[i % 2]
        for kc in range(NC):
            p.mm(ps[:, :], AR[kc][:, i * 128:(i + 1) * 128],
                 scrb[kc // 4][:, (kc % 4) * 512:(kc % 4 + 1) * 512],
                 start=(kc == 0), stop=(kc == NC - 1))
        p.copy(AR8[i // 4][:, (i % 4) * 512:(i % 4 + 1) * 512], ps[:, :])
    knull = com.wg8[:, 48:52]
    for hc in range(4):
        p.dma("gpsimd", knull[:, hc:hc + 1],
              io["L3_ffn_null_kv"][0, 2 * hc:2 * hc + 2, :]
              .rearrange("h d -> (h d)").rearrange("(a b) -> a b", b=1))
    p.dma("gpsimd", com.rrow[0:1, 0:512],
          io["L3_ffn_null_kv"][1].rearrange("h d -> (h d)")
          .rearrange("(o c) -> o c", o=1))
    bc3b = com.bc3.bitcast(BF16)

    for hh in range(H):
        hc, hr = divmod(hh, 2)
        qf = AR[8 + hc]
        kf = AR[12 + hc]
        for qs in range(4):
            out_ps = psO[qs % 2]
            for kt in range(NT):
                sc_ps = psA[kt % 2]
                p.mm(sc_ps[:, :], kf[hr * 64:hr * 64 + 64, kt * 128:(kt + 1) * 128],
                     qf[hr * 64:hr * 64 + 64, qs * 512:(qs + 1) * 512],
                     start=True, stop=True)
                acol = com.small[:, 6 + kt % 2:7 + kt % 2]
                p.red(acol, sc_ps[:, :], ALU.max, negate=True)
                sb_a = com.small[:, 8 + kt % 2:9 + kt % 2]
                p.ts(sb_a, acol, SCL, None, ALU.mult)
                ppt = com.pp[kt % 2]
                p.act(ppt[:, :], sc_ps[:, :], AF.Exp, bias=sb_a, scale=SCL)
                expa = com.small[:, 10 + kt % 2:11 + kt % 2]
                p.act(expa, acol, AF.Exp, scale=-SCL)
                vt = com.vv[kt % 2]
                p.ts(vt[:, 0:64], vl_ap(kt, hh), expa, None, ALU.mult)
                p.copy(vt[:, 64:65], expa)
                p.mm(out_ps[0:65, :], vt[:, 0:65], ppt[:, :],
                     start=(kt == 0), stop=False)
            p.mm(psA[2][0:1, :], knull[hr * 64:hr * 64 + 64, hc:hc + 1],
                 qf[hr * 64:hr * 64 + 64, qs * 512:(qs + 1) * 512],
                 start=True, stop=True)
            an = com.small[0:1, 12:13]
            p.red(an, psA[2][0:1, :], ALU.max, negate=True)
            sbn = com.small[0:1, 13:14]
            p.ts(sbn, an, SCL, None, ALU.mult)
            ppn = bc3b[0:1, 0:512]
            p.act(ppn, psA[2][0:1, :], AF.Exp, bias=sbn, scale=SCL)
            expn = com.small[0:1, 14:15]
            p.act(expn, an, AF.Exp, scale=-SCL)
            vn = bc3b[0:1, 512:577]
            p.ts(vn[0:1, 0:64], com.rrow[0:1, hh * 64:(hh + 1) * 64], expn,
                 None, ALU.mult)
            p.copy(vn[0:1, 64:65], expn)
            p.mm(out_ps[0:65, :], vn[0:1, 0:65], ppn, start=False, stop=True)
            rz = com.rowbuf[0:1, 0:512]
            p.recip(rz, out_ps[64:65, 0:512])
            p.mm(psA[2][hr * 64:hr * 64 + 64, 0:512], com.ones_rowf[0:1, 0:64],
                 rz, start=True, stop=True)
            rbs = com.bc2[hr * 64:hr * 64 + 64, 0:512]
            p.copy(rbs, psA[2][hr * 64:hr * 64 + 64, 0:512], eng="scalar")
            p.tt(AR8[4 + hc][hr * 64:hr * 64 + 64, qs * 512:(qs + 1) * 512],
                 out_ps[0:64, :], rbs, ALU.mult)

    for kc in range(4):
        p.dma("gpsimd", AR[kc // 2][:, (kc % 2) * 1024:(kc % 2 + 1) * 1024],
              io["L3_ffn_w_out_h"][kc * 128:(kc + 1) * 128, :])
    for i in range(NT):
        mcol = com.gates[:, 32 + i:32 + i + 1]
        for half in range(2):
            ps = psO[half]
            for kc in range(4):
                p.mm(ps[:, :], AR8[4 + kc][:, i * 128:(i + 1) * 128],
                     AR[kc // 2][:, (kc % 2) * 1024 + half * 512:
                                 (kc % 2) * 1024 + (half + 1) * 512],
                     start=(kc == 0), stop=(kc == 3))
            sl = slice(half * 512, (half + 1) * 512)
            t = com.bc2[:, sl]
            p.tt(t, ps[:, :], com.bc1[:, sl], ALU.subtract)
            p.stt(t, t, mcol, com.bc1[:, sl], ALU.mult, ALU.add)
            p.tt(com.GA[i][:, sl], com.GA[i][:, sl], t, ALU.add)


def emit_l3_attn(p, com, io, seq, x_h, y_h):
    nc = p.nc
    psA, psO, psT = com.psA, com.psO, com.psT
    XN = com.XN
    AR, AR8 = com.AR, com.AR8
    SC3 = 96.0 ** -0.5
    scrb = [com.scr[0][:, :].bitcast(BF16), com.scr[1][:, :].bitcast(BF16)]
    rope_h = declare_in(nc, "c_rope", (128, T))

    for mi in range(2):
        for kc in range(NC):
            wc = com.wcol[kc % 2]
            p.dma("gpsimd", wc[:, :],
                  io["L3_attn_W_DKV"][kc * 128:(kc + 1) * 128, mi * 128:(mi + 1) * 128])
            for ni in range(4):
                p.mm(psA[ni][:, :], wc[:, :], XN[kc][:, ni * 512:(ni + 1) * 512],
                     start=(kc == 0), stop=(kc == NC - 1))
        for ni in range(4):
            p.copy(scrb[mi][:, ni * 512:(ni + 1) * 512], psA[ni][:, :])

    for half in range(2):
        p.dma("gpsimd", com.pp[0][:, :], rope_h[:, half * 1024:half * 1024 + 512])
        p.dma("gpsimd", com.pp[1][:, :], rope_h[:, half * 1024 + 512:half * 1024 + 1024])
        for base in (0, 64):
            for kc in range(NC):
                wc = com.wcol[kc % 2]
                p.memset(wc[:, 0:64], 0.0)
                p.dma("gpsimd", wc[:, 0:32],
                      io["L3_attn_W_KR"][kc * 128:(kc + 1) * 128, :])
                for ni in range(2):
                    p.mm(psA[ni][base:base + 64, :], wc[:, 0:64],
                         XN[kc][:, half * 1024 + ni * 512:half * 1024 + (ni + 1) * 512],
                         start=(kc == 0), stop=(kc == NC - 1))
            for ni in range(2):
                p.tt(com.wstr[half][base:base + 64, ni * 512:(ni + 1) * 512],
                     psA[ni][base:base + 64, :],
                     com.pp[ni][base:base + 64, :], ALU.mult)

    def cq_ap(kc32, nslice):
        return AR[kc32 // 2][:, (kc32 % 2) * 1024 + nslice * 512:
                             (kc32 % 2) * 1024 + (nslice + 1) * 512]

    def emit_cq(half):
        for mi in range(32):
            for kc in range(NC):
                wc = com.wcol[kc % 2]
                p.dma("gpsimd", wc[:, :],
                      io["L3_attn_W_DQ"][kc * 128:(kc + 1) * 128, mi * 128:(mi + 1) * 128])
                for ni in range(2):
                    p.mm(psA[ni][:, :], wc[:, :],
                         XN[kc][:, half * 1024 + ni * 512:half * 1024 + (ni + 1) * 512],
                         start=(kc == 0), stop=(kc == NC - 1))
            for ni in range(2):
                p.copy(AR[mi // 2][:, (mi % 2) * 1024 + ni * 512:
                                   (mi % 2) * 1024 + (ni + 1) * 512], psA[ni][:, :])

    def emit_qc(half):
        for mi in range(NC):
            for kc in range(32):
                wc = com.wcol[kc % 2]
                p.dma("gpsimd", wc[:, :],
                      io["L3_attn_W_UQ"][kc * 128:(kc + 1) * 128, mi * 128:(mi + 1) * 128])
                for ni in range(2):
                    p.mm(psA[ni][:, :], wc[:, :], cq_ap(kc, ni),
                         start=(kc == 0), stop=(kc == 31))
            for ni in range(2):
                p.copy(AR8[mi][:, half * 1024 + ni * 512:half * 1024 + (ni + 1) * 512],
                       psA[ni][:, :])

    def emit_qr(half, dst_fn):
        p.dma("gpsimd", com.pp[0][:, :], rope_h[:, half * 1024:half * 1024 + 512])
        p.dma("gpsimd", com.pp[1][:, :], rope_h[:, half * 1024 + 512:half * 1024 + 1024])
        for mi in range(8):
            for kc in range(32):
                wc = com.wcol[kc % 2]
                p.memset(wc[:, :], 0.0)
                p.dma("gpsimd", wc[:, 0:32],
                      io["L3_attn_W_QR"][kc * 128:(kc + 1) * 128, mi * 64:mi * 64 + 32])
                p.dma("gpsimd", wc[:, 64:96],
                      io["L3_attn_W_QR"][kc * 128:(kc + 1) * 128,
                                         mi * 64 + 32:mi * 64 + 64])
                for ni in range(2):
                    p.mm(psA[2 + ni][:, :], wc[:, :], cq_ap(kc, ni),
                         start=(kc == 0), stop=(kc == 31))
            for ni in range(2):
                p.tt(dst_fn(mi, ni), psA[2 + ni][:, :], com.pp[ni][:, :], ALU.mult)

    emit_cq(0)
    emit_qc(0)
    emit_qr(0, lambda mi, ni: AR8[mi][:, 1024 + ni * 512:1024 + (ni + 1) * 512])
    emit_cq(1)
    for mi in range(8):
        p.copy(XN[mi][:, 0:1024], AR8[mi][:, 1024:2048])
    emit_qc(1)
    emit_qr(1, lambda mi, ni: XN[mi][:, 1024 + ni * 512:1024 + (ni + 1) * 512])

    for mi in range(NC):
        for kc in range(2):
            wc = com.wcol[kc % 2]
            p.dma("gpsimd", wc[:, :],
                  io["L3_attn_W_UK"][kc * 128:(kc + 1) * 128, mi * 128:(mi + 1) * 128])
            for ni in range(4):
                p.mm(psA[ni][:, :], wc[:, :], scrb[kc][:, ni * 512:(ni + 1) * 512],
                     start=(kc == 0), stop=(kc == 1))
        for ni in range(4):
            p.copy(AR[mi][:, ni * 512:(ni + 1) * 512], psA[ni][:, :])
    wuv = com.bc2.bitcast(BF16)
    for kc in range(2):
        p.dma("gpsimd", wuv[:, kc * 1024:(kc + 1) * 1024],
              io["L3_attn_W_UV"][kc * 128:(kc + 1) * 128, :])
    for i in range(NT):
        for half2 in range(2):
            ps = psA[i % 2 * 2 + half2]
            for kc in range(2):
                p.mm(ps[:, :], scrb[kc][:, i * 128:(i + 1) * 128],
                     wuv[:, kc * 1024 + half2 * 512:kc * 1024 + (half2 + 1) * 512],
                     start=(kc == 0), stop=(kc == 1))
            p.copy(AR[8 + i // 2][:, (i % 2) * 1024 + half2 * 512:
                                  (i % 2) * 1024 + (half2 + 1) * 512], ps[:, :])

    def v_ap(kt, h):
        return AR[8 + kt // 2][:, (kt % 2) * 1024 + h * 64:(kt % 2) * 1024 + (h + 1) * 64]

    for h in range(16):
        hc, hr = divmod(h, 2)
        for qs in range(4):
            out_ps = psO[qs % 2]
            nkt = 4 * (qs + 1)
            for kt in range(nkt):
                sc_ps = psA[kt % 2]
                p.mm(sc_ps[:, :], AR[hc][hr * 64:hr * 64 + 64, kt * 128:(kt + 1) * 128],
                     AR8[hc][hr * 64:hr * 64 + 64, qs * 512:(qs + 1) * 512],
                     start=True, stop=False)
                p.mm(sc_ps[:, :],
                     com.wstr[kt // 8][hr * 64:hr * 64 + 64,
                                       (kt % 8) * 128:(kt % 8 + 1) * 128],
                     XN[hc][hr * 64:hr * 64 + 64, qs * 512:(qs + 1) * 512],
                     start=False, stop=True)
                acol = com.small[:, 6 + kt % 2:7 + kt % 2]
                p.red(acol, sc_ps[:, :], ALU.max, negate=True)
                sb_a = com.small[:, 8 + kt % 2:9 + kt % 2]
                p.ts(sb_a, acol, SC3, None, ALU.mult)
                ppt = com.pp[kt % 2]
                p.act(ppt[:, :], sc_ps[:, :], AF.Exp, bias=sb_a, scale=SC3)
                qlo, klo = qs * 512, kt * 128
                if klo + 127 >= qlo:
                    off = klo - qlo
                    p.tt(ppt[:, :], ppt[:, :], com.band[:, 384 - off:896 - off],
                         ALU.mult)
                expa = com.small[:, 10 + kt % 2:11 + kt % 2]
                p.act(expa, acol, AF.Exp, scale=-SC3)
                vt = com.vv[kt % 2]
                p.ts(vt[:, 0:64], v_ap(kt, h), expa, None, ALU.mult)
                p.copy(vt[:, 64:65], expa)
                p.mm(out_ps[0:65, :], vt[:, 0:65], ppt[:, :],
                     start=(kt == 0), stop=(kt == nkt - 1))
            rz = com.rowbuf[0:1, 0:512]
            p.recip(rz, out_ps[64:65, 0:512])
            p.mm(psA[2][hr * 64:hr * 64 + 64, 0:512], com.ones_rowf[0:1, 0:64],
                 rz, start=True, stop=True)
            rbs = com.bc1[hr * 64:hr * 64 + 64, 0:512]
            p.copy(rbs, psA[2][hr * 64:hr * 64 + 64, 0:512], eng="scalar")
            p.tt(AR8[hc][hr * 64:hr * 64 + 64, qs * 512:(qs + 1) * 512],
                 out_ps[0:64, :], rbs, ALU.mult)

    for kc in range(NC):
        p.dma("gpsimd", AR[kc // 2][:, (kc % 2) * 1024:(kc % 2 + 1) * 1024],
              io["L3_attn_W_O"][kc * 128:(kc + 1) * 128, :])
    for i in range(NT):
        for half in range(2):
            ps = psO[half]
            for kc in range(NC):
                p.mm(ps[:, :], AR8[kc][:, i * 128:(i + 1) * 128],
                     AR[kc // 2][:, (kc % 2) * 1024 + half * 512:
                                 (kc % 2) * 1024 + (half + 1) * 512],
                     start=(kc == 0), stop=(kc == NC - 1))
        s_f = com.scr[0]
        for half in range(2):
            sl = slice(half * 512, (half + 1) * 512)
            p.copy(s_f[:, sl], psO[half][:, :])
        emit_final_update(p, com, x_h, y_h, seq, i, s_f[:, :])


def emit_layer3(p, com, io, seq):
    emit_ln_stage(p, com, io["x"], seq)
    emit_l3_ffn(p, com, io, seq)
    emit_l3_attn(p, com, io, seq, io["x"], io["y"])


L3_INPUT_NAMES = [
    "L3_attn_W_DQ", "L3_attn_W_UQ", "L3_attn_W_DKV", "L3_attn_W_UK",
    "L3_attn_W_UV", "L3_attn_W_QR", "L3_attn_W_KR", "L3_attn_W_O",
    "L3_ffn_ln_g", "L3_ffn_ln_b", "L3_ffn_w_qkv_l", "L3_ffn_w_out_l",
    "L3_ffn_null_q", "L3_ffn_rt_q", "L3_ffn_rt_kv", "L3_ffn_rms_g",
    "L3_ffn_w_q_h", "L3_ffn_w_kv_h", "L3_ffn_null_kv", "L3_ffn_w_out_h"]

L3_SHAPES = {
    "L3_attn_W_DQ": (C, 4096), "L3_attn_W_UQ": (4096, C),
    "L3_attn_W_DKV": (C, 256), "L3_attn_W_UK": (256, C),
    "L3_attn_W_UV": (256, C), "L3_attn_W_QR": (4096, 512),
    "L3_attn_W_KR": (C, 32), "L3_attn_W_O": (C, C),
    "L3_ffn_ln_g": (C,), "L3_ffn_ln_b": (C,), "L3_ffn_w_qkv_l": (C, 1536),
    "L3_ffn_w_out_l": (512, C), "L3_ffn_null_q": (C,),
    "L3_ffn_rt_q": (1, C), "L3_ffn_rt_kv": (1, C), "L3_ffn_rms_g": (C,),
    "L3_ffn_w_q_h": (C, 512), "L3_ffn_w_kv_h": (C, 8, 128),
    "L3_ffn_null_kv": (2, 8, 64), "L3_ffn_w_out_h": (512, C)}


def build_layer3_program():
    global _DECL_CACHE
    _DECL_CACHE = {}
    nc = bass.Bass("TRN2", target_bir_lowering=False, debug=False)
    p = Prog(nc)
    com = Com(p)
    io = {"x": nc.dram_tensor("x", [NSEQ, T, C], F32, kind="ExternalInput").ap(),
          "y": declare_out(nc, "y", (NSEQ, T, C)),
          "l3scr": nc.dram_tensor("l3scr", [4096], F32, kind="Internal").ap()}
    for n in L3_INPUT_NAMES:
        io[n] = declare_in(nc, n, L3_SHAPES[n])
    for seq in range(NSEQ):
        emit_layer3(p, com, io, seq)
    p.build()
    return nc, p


def run_layer3(x, weights):
    nc, p = build_layer3_program()
    cst = make_constants()
    in_maps = []
    for core in range(8):
        m = {"x": np.ascontiguousarray(x[core * 2:(core + 1) * 2])}
        for k, v in weights.items():
            m[k] = np.asarray(v, dtype=np.float32)
        m.update(cst)
        in_maps.append(m)
    res = run_bass_kernel_spmd(nc, in_maps, core_ids=list(range(8)))
    out = np.concatenate([r["y"] for r in res.results], axis=0)
    p.stack.close()
    return out



def emit_l0_ffn(p, com, io, seq):
    nc = p.nc
    psA, psO, psT = com.psA, com.psO, com.psT
    XN = com.XN
    AR, AR8 = com.AR, com.AR8

    wg_sb = com.wg8
    p.dma("gpsimd", wg_sb[:, 0:64].rearrange("p (c e) -> p c e", e=8),
          io["L0_ffn_router_DE"].rearrange("(c p) e -> p c e", p=128))
    for i in range(NT):
        ps = psA[i % 2]
        for kc in range(NC):
            p.mm(ps[:, 0:8], XN[kc][:, i * 128:(i + 1) * 128],
                 wg_sb[:, kc * 8:(kc + 1) * 8],
                 start=(kc == 0), stop=(kc == NC - 1))
        lg = com.scr[0][:, 0:8]
        p.copy(lg, ps[:, 0:8])
        gmat = com.scr[0][:, 16:80]
        in0 = bass.AP(lg.tensor, lg.offset, [list(lg.ap[0]), [0, 8], [1, 8]])
        in1 = bass.AP(lg.tensor, lg.offset, [list(lg.ap[0]), [1, 8], [0, 8]])
        p.tt(gmat.rearrange("p (a b) -> p a b", b=8), in1, in0, ALU.is_lt)
        cnt = com.scr[0][:, 8:16]
        p.red(cnt, gmat.rearrange("p (a b) -> p a b", b=8), ALU.add)
        msk = com.scr[0][:, 80:88]
        p.ts(msk, cnt, 1.5, None, ALU.is_lt)
        sg = com.scr[0][:, 88:96]
        p.act(sg, lg, AF.Sigmoid)
        p.tt(com.gates[:, 8 * i:8 * i + 8], sg, msk, ALU.mult)

    for fc in range(16):
        f0 = fc * 128
        for wi, w_h in enumerate((io["L0_ffn_w_in_shared_FD"],
                                  io["L0_ffn_w_swiglu_FD"])):
            stg = com.scr[wi][:, :].bitcast(BF16)
            p.dma("gpsimd", stg[:, 0:1024], w_h[f0:f0 + 128, :])
            for kc in range(NC):
                ps = psT[kc % 2]
                p.transpose(ps[:, :], stg[:, kc * 128:(kc + 1) * 128], com.ident[:, :])
                p.copy(com.wstr[wi][:, kc * 128:(kc + 1) * 128], ps[:, :],
                       eng="vector" if kc % 2 == 0 else "scalar")
        for ni in range(4):
            nis = slice(ni * 512, (ni + 1) * 512)
            for kc in range(NC):
                p.mm(psA[0][:, :], com.wstr[0][:, kc * 128:(kc + 1) * 128],
                     XN[kc][:, nis], start=(kc == 0), stop=(kc == NC - 1))
                p.mm(psA[1][:, :], com.wstr[1][:, kc * 128:(kc + 1) * 128],
                     XN[kc][:, nis], start=(kc == 0), stop=(kc == NC - 1))
            sl = com.pp[ni % 2]
            p.act(sl[:, :], psA[0][:, :], AF.Silu)
            p.tt(AR[fc][:, nis], sl[:, :], psA[1][:, :], ALU.mult)
    for fc in range(16):
        f0 = fc * 128
        for kc in range(NC):
            wcl = com.wcol[kc % 2]
            p.dma("gpsimd", wcl[:, :],
                  io["L0_ffn_w_out_shared_DF"][kc * 128:(kc + 1) * 128, f0:f0 + 128])
            ps = psT[kc % 2]
            p.transpose(ps[:, :], wcl[:, :], com.ident[:, :])
            p.copy(AR8[fc // 2][:, (fc % 2) * 1024 + kc * 128:
                               (fc % 2) * 1024 + (kc + 1) * 128], ps[:, :],
                   eng="vector" if kc % 2 == 0 else "scalar")
    for i in range(NT):
        for dh in range(2):
            ps = psO[dh]
            for fc in range(16):
                p.mm(ps[:, :], AR[fc][:, i * 128:(i + 1) * 128],
                     AR8[fc // 2][:, (fc % 2) * 1024 + dh * 512:
                                  (fc % 2) * 1024 + (dh + 1) * 512],
                     start=(fc == 0), stop=(fc == 15))
            p.copy(com.GA[i][:, dh * 512:(dh + 1) * 512], ps[:, :])

    for e in range(8):
        for i in range(NT):
            gcolb = com.wcol[0][:, 0:1]
            p.copy(gcolb, com.gates[:, 8 * i + e:8 * i + e + 1])
            p.mm(psA[3][0:1, 0:128], gcolb, com.ident[:, :], start=True, stop=True)
            p.copy(com.rowbuf[0:1, 0:128], psA[3][0:1, 0:128])
            dst = (com.bc2 if i < 8 else com.bc3)[:, (i % 8) * 128:(i % 8 + 1) * 128]
            com.row_to_bc(com.rowbuf[0:1, 0:128], dst, 128)
        for kc in range(NC):
            p.tt(AR8[kc][:, 0:1024], XN[kc][:, 0:1024], com.bc2[:, :], ALU.mult)
            p.tt(AR8[kc][:, 1024:2048], XN[kc][:, 1024:2048], com.bc3[:, :], ALU.mult)
        for fc in range(16):
            f0 = fc * 128
            p.dma("gpsimd", com.wstr[0][:, :].rearrange("p (c f) -> p c f", f=128),
                  io["L0_ffn_w_in_eDF"][e].rearrange("(c p) f -> p c f", p=128)[:, :, f0:f0 + 128])
            p.dma("gpsimd", com.wstr[1][:, :].rearrange("p (c f) -> p c f", f=128),
                  io["L0_ffn_w_swiglu_eDF"][e].rearrange("(c p) f -> p c f", p=128)[:, :, f0:f0 + 128])
            for ni in range(4):
                nis = slice(ni * 512, (ni + 1) * 512)
                for kc in range(NC):
                    p.mm(psA[0][:, :], com.wstr[0][:, kc * 128:(kc + 1) * 128],
                         AR8[kc][:, nis], start=(kc == 0), stop=(kc == NC - 1))
                    p.mm(psA[1][:, :], com.wstr[1][:, kc * 128:(kc + 1) * 128],
                         AR8[kc][:, nis], start=(kc == 0), stop=(kc == NC - 1))
                sl = com.pp[ni % 2]
                p.act(sl[:, :], psA[0][:, :], AF.Silu)
                p.tt(AR[fc][:, nis], sl[:, :], psA[1][:, :], ALU.mult)
        for fc in range(16):
            p.dma("gpsimd", AR8[fc // 2][:, (fc % 2) * 1024:(fc % 2 + 1) * 1024],
                  io["L0_ffn_w_out_eFD"][e, fc * 128:(fc + 1) * 128, :])
        for i in range(NT):
            for dh in range(2):
                ps = psO[dh]
                for fc in range(16):
                    p.mm(ps[:, :], AR[fc][:, i * 128:(i + 1) * 128],
                         AR8[fc // 2][:, (fc % 2) * 1024 + dh * 512:
                                      (fc % 2) * 1024 + (dh + 1) * 512],
                         start=(fc == 0), stop=(fc == 15))
                sl_ = slice(dh * 512, (dh + 1) * 512)
                p.tt(com.GA[i][:, sl_], com.GA[i][:, sl_], ps[:, :], ALU.add)


def emit_l0_attn(p, com, io, seq, x_h, y_h):
    nc = p.nc
    psA, psO, psT = com.psA, com.psO, com.psT
    XN = com.XN
    AR, AR8 = com.AR, com.AR8
    scrb = [com.scr[0][:, :].bitcast(BF16), com.scr[1][:, :].bitcast(BF16)]
    bc2b = com.bc2.bitcast(BF16)
    bc3b = com.bc3.bitcast(BF16)
    yscr = io["l0y"]

    for qk, (w_h, dstbase, tokw) in enumerate(
            ((io["L0_attn_Wq"], 0, bc3b), (io["L0_attn_Wk"], 8, bc2b))):
        p.dma("gpsimd", scrb[0][:, 0:1024], w_h[0:128, :])
        p.dma("gpsimd", scrb[0][:, 1024:2048], w_h[128:256, :])
        for kc in range(NC):
            for rc in range(2):
                ps = psT[rc]
                p.transpose(ps[:, :], scrb[0][:, rc * 1024 + kc * 128:
                                              rc * 1024 + (kc + 1) * 128],
                            com.ident[:, :])
                p.copy(tokw[:, kc * 256 + rc * 128:kc * 256 + (rc + 1) * 128],
                       ps[:, :], eng="scalar")
                for m in range(4):
                    mt = rc * 4 + m
                    wcl = AR8[mt // 2][:, (mt % 2) * 1024 + kc * 128:
                                       (mt % 2) * 1024 + (kc + 1) * 128]
                    p.memset(wcl, 0.0)
                    p.copy(wcl[:, 0:16], ps[:, (2 * m) * 16:(2 * m) * 16 + 16])
                    p.copy(wcl[:, 64:80],
                           ps[:, (2 * m + 1) * 16:(2 * m + 1) * 16 + 16])
        for mt in range(8):
            for kc in range(NC):
                wcl = AR8[mt // 2][:, (mt % 2) * 1024 + kc * 128:
                                   (mt % 2) * 1024 + (kc + 1) * 128]
                for ni in range(4):
                    p.mm(psA[ni][:, :], wcl, XN[kc][:, ni * 512:(ni + 1) * 512],
                         start=(kc == 0), stop=(kc == NC - 1))
            for ni in range(4):
                p.copy(AR[dstbase + mt][:, ni * 512:(ni + 1) * 512], psA[ni][:, :])
    qkscr = io["l0qk"]
    for i in range(NT):
        for qk, tokw in ((0, bc3b), (1, bc2b)):
            ps = psA[qk]
            for kc in range(NC):
                p.mm(ps[:, 0:256], XN[kc][:, i * 128:(i + 1) * 128],
                     tokw[:, kc * 256:(kc + 1) * 256],
                     start=(kc == 0), stop=(kc == NC - 1))
            stg = com.lmask if qk == 0 else com.band[:, 0:256]
            p.copy(stg[:, 0:256] if qk == 0 else stg, ps[:, 0:256])
            p.dma("sync", qkscr[seq, qk, i * 128:(i + 1) * 128, :],
                  stg[:, 0:256] if qk == 0 else stg)

    for dh in range(2):
        for kc in range(NC):
            for m in range(4):
                mg = dh * 4 + m
                wcl = com.wcol[m % 2]
                p.dma("gpsimd", wcl[:, :],
                      io["L0_attn_Wv"][mg * 128:(mg + 1) * 128,
                                       kc * 128:(kc + 1) * 128])
                ps = psT[m % 2]
                p.transpose(ps[:, :], wcl[:, :], com.ident[:, :])
                p.copy(scrb[kc // 4][:, (kc % 4) * 512 + m * 128:
                                     (kc % 4) * 512 + (m + 1) * 128], ps[:, :],
                       eng="vector" if m % 2 == 0 else "scalar")
        for i in range(NT):
            ps = psA[i % 2]
            for kc in range(NC):
                p.mm(ps[:, :], XN[kc][:, i * 128:(i + 1) * 128],
                     scrb[kc // 4][:, (kc % 4) * 512:(kc % 4 + 1) * 512],
                     start=(kc == 0), stop=(kc == NC - 1))
            if i < 8:
                p.copy(AR8[4 + i // 2][:, (i % 2) * 1024 + dh * 512:
                                       (i % 2) * 1024 + (dh + 1) * 512], ps[:, :])
            else:
                j = i - 8
                p.copy(AR8[j // 2][:, (j % 2) * 1024 + dh * 512:
                                   (j % 2) * 1024 + (dh + 1) * 512], ps[:, :])

    def v_ap(i, h):
        base = h * 64
        if i < 8:
            return AR8[4 + i // 2][:, (i % 2) * 1024 + base:(i % 2) * 1024 + base + 64]
        j = i - 8
        return AR8[j // 2][:, (j % 2) * 1024 + base:(j % 2) * 1024 + base + 64]

    import os
    cut = os.environ.get("L0_CUT", "all")
    p.dma("gpsimd", com.rrow[0:1, 0:64],
          io["L0_attn_g_weight"].rearrange("(o c) -> o c", o=1))
    gwt = com.gates[:, 64:128]
    com.row_to_bc(com.rrow[0:1, 0:64], gwt, 64)
    p.memset(com.wg8[:, 32:33], 1.0)

    S2f = XN[0][:, :].bitcast(F32)
    S1f = XN[1][:, :].bitcast(F32)
    S0f = XN[1][:, :].bitcast(F32)
    S2b = XN[2]
    S1b = XN[3]

    if cut == "proj":
        for i in range(NT):
            src_ap = (AR8[4 + i // 2][:, (i % 2) * 1024:(i % 2 + 1) * 1024]
                      if i < 8 else
                      AR8[(i - 8) // 2][:, ((i - 8) % 2) * 1024:((i - 8) % 2 + 1) * 1024])
            p.dma("sync", yscr[seq, i * 128:(i + 1) * 128, :], src_ap)
    step = int(os.environ.get("L0_STEP", "9"))
    ngroups = 0 if cut == "proj" else 2
    for which, name in ((0, "c_e1"), (1, "c_e2")):
        dh_ = declare_in(nc, name, (64, 256))
        p.dma("gpsimd", com.pp[1][0:64, which * 256:(which + 1) * 256], dh_)
        p.dma("gpsimd", com.pp[1][64:128, which * 256:(which + 1) * 256], dh_)
    for hg in range(ngroups):
        p.memset(S2f[:, :], 0.0)
        p.memset(S1f[0:16, 0:512], 0.0)
        p.memset(S1f[0:1, 512:1024], 0.0)
        p.memset(S2b[:, 0:1024], 0.0)
        p.memset(S1b[:, 0:1536], 0.0)
        for i in range(NT):
            p.dma("gpsimd", com.band[:, 0:256], qkscr[seq, 1, i * 128:(i + 1) * 128, :])
            if step >= 3:
                for par in (0, 1):
                    jhs = [j for j in range(8) if (hg * 8 + j) % 2 == par]
                    for rc in range(2):
                        for n_, jh in enumerate(jhs):
                            h = hg * 8 + jh
                            hr = h % 2
                            qsl = AR[h // 2][hr * 64:hr * 64 + 64,
                                             i * 128:(i + 1) * 128]
                            p.mm(psA[2][:, n_ * 128:(n_ + 1) * 128],
                                 com.pp[1][hr * 64:hr * 64 + 64,
                                           rc * 128:rc * 128 + 128],
                                 qsl, start=True, stop=True)
                            p.mm(psA[3][:, n_ * 128:(n_ + 1) * 128],
                                 com.pp[1][hr * 64:hr * 64 + 64,
                                           256 + rc * 128:256 + rc * 128 + 128],
                                 qsl, start=True, stop=True)
                        for n_, jh in enumerate(jhs):
                            r1 = com.pp[0][:, 0:128]
                            p.copy(r1, psA[2][:, n_ * 128:(n_ + 1) * 128])
                            p.tt(com.wstr[rc][:, jh * 128:(jh + 1) * 128],
                                 psA[3][:, n_ * 128:(n_ + 1) * 128], r1, ALU.mult)
            if step < 2:
                continue
            for jh in [0, 2, 4, 6, 1, 3, 5, 7]:
                h = hg * 8 + jh
                hr = h % 2
                p.mm(psA[jh // 4][:, (jh % 4) * 128:(jh % 4 + 1) * 128],
                     AR[8 + h // 2][hr * 64:hr * 64 + 64, i * 128:(i + 1) * 128],
                     AR[h // 2][hr * 64:hr * 64 + 64, i * 128:(i + 1) * 128],
                     start=True, stop=True)
            bnd = com.band[:, :]
            trilrep = bass.AP(bnd.tensor, bnd.offset + 384,
                              [list(bnd.ap[0]), [0, 4], [1, 128]])
            if os.environ.get("L0_POLY", "1") == "0":
                for b in range(2):
                    p.copy(XN[5][:, b * 512:(b + 1) * 512], psA[b][:, :])
                continue
            for b in range(2):
                sq = com.pp[0]
                sqn = p.act(sq[:, :], psA[b][:, :], AF.Square)
                t1 = com.bc1[:, b * 512:(b + 1) * 512]
                p.ts(t1, psA[b][:, :], 0.25, 1.0, ALU.mult, ALU.add,
                     extra_dep=sqn)
                asl = XN[5][:, b * 512:(b + 1) * 512]
                p.stt(asl, sq[:, :], 1.0 / 32.0, t1, ALU.mult, ALU.add)
                p.tt(asl, asl, trilrep, ALU.mult)
            if step < 3:
                continue
            if os.environ.get("L0_Y", "1") == "0":
                continue
            first = [True]

            def ymm(out_sl, lhsT, rhs, last=False):
                p.mm(out_sl, lhsT, rhs, start=first[0], stop=last)
                first[0] = False

            for jh in range(8):
                h = hg * 8 + jh
                osl = psO[0][:, jh * 64:(jh + 1) * 64]
                ymm(osl, AR[h // 2][:, i * 128:(i + 1) * 128],
                    XN[3][:, jh * 64:(jh + 1) * 64])
                for rc in range(2):
                    ymm(osl, com.wstr[rc][:, jh * 128:(jh + 1) * 128],
                        XN[2][:, (2 * jh + rc) * 64:(2 * jh + rc + 1) * 64])
                ymm(osl, XN[5][:, (jh // 4) * 512 + (jh % 4) * 128:
                               (jh // 4) * 512 + (jh % 4) * 128 + 128],
                    v_ap(i, h), last=(jh == 7))
            p.mm(psA[2][:, 0:512], com.ones_rowf[0:1, :],
                 S0f[0:1, 512:1024], start=True, stop=True)
            s0bc = com.bc1[:, 512:1024]
            p.copy(s0bc, psA[2][:, 0:512], eng="scalar")
            ysum = com.bc1[:, 0:512]
            p.tt(ysum, psO[0][:, :], s0bc, ALU.add)
            if step < 4:
                continue
            st2 = com.band[:, 0:256]
            for jh in range(8):
                h = hg * 8 + jh
                sl = bass.AP(st2.tensor, st2.offset + h * 16,
                             [list(st2.ap[0]), [1, 16], [0, 16]])
                sr = bass.AP(st2.tensor, st2.offset + h * 16,
                             [list(st2.ap[0]), [0, 16], [1, 16]])
                p.tt(XN[6][:, jh * 256:(jh + 1) * 256]
                     .rearrange("p (a b) -> p a b", b=16), sl, sr, ALU.mult)
            if step < 5:
                continue
            for jh in range(8):
                h = hg * 8 + jh
                hr = h % 2
                p.mm(psA[2][0:1, 0:64], com.wg8[:, 32:33], v_ap(i, h),
                     start=True, stop=True)
                s0slot = S0f[0:1, 512 + jh * 64:512 + (jh + 1) * 64]
                p.tt(s0slot, s0slot, psA[2][0:1, 0:64], ALU.add)
                kt16 = bass.AP(st2.tensor, st2.offset + h * 16,
                               [list(st2.ap[0]), [1, 16]])
                p.mm(psA[2][0:16, 64:128], kt16, v_ap(i, h), start=True, stop=True)
                s1slot = S1f[0:16, jh * 64:(jh + 1) * 64]
                p.stt(s1slot, psA[2][0:16, 64:128], 0.25, s1slot, ALU.mult, ALU.add)
                p.copy(XN[3][hr * 64:hr * 64 + 16, jh * 64:(jh + 1) * 64], s1slot)
                for rc in range(2):
                    p.mm(psA[3][:, rc * 64:(rc + 1) * 64],
                         XN[6][:, jh * 256 + rc * 128:jh * 256 + (rc + 1) * 128],
                         v_ap(i, h), start=True, stop=True)
                    s2slot = S2f[:, (2 * jh + rc) * 64:(2 * jh + rc + 1) * 64]
                    p.stt(s2slot, psA[3][:, rc * 64:(rc + 1) * 64], 1.0 / 32.0,
                          s2slot, ALU.mult, ALU.add)
                    p.copy(XN[2][:, (2 * jh + rc) * 64:(2 * jh + rc + 1) * 64],
                           s2slot)
            if step < 6:
                continue
            ysum = com.bc1[:, 0:512]
            sq = com.pp[0]
            p.act(sq[:, :], ysum, AF.Square)
            ssum = com.small[:, 12:20]
            p.red(ssum, sq[:, :].rearrange("p (a b) -> p a b", b=64), ALU.add)
            rms = com.small[:, 20:28]
            p.ts(rms, ssum, 1.0 / 64.0, None, ALU.mult)
            p.act(rms, rms, AF.Ln, bias=com.ceps[:, 0:1])
            p.act(rms, rms, AF.Exp, scale=-0.5)
            rmsbc = bass.AP(rms.tensor, rms.offset, [list(rms.ap[0]), [1, 8], [0, 64]])
            yt = ysum
            p.tt(yt, ysum, rmsbc, ALU.mult)
            gwbc = bass.AP(gwt.tensor, gwt.offset, [list(gwt.ap[0]), [0, 8], [1, 64]])
            p.tt(yt, yt, gwbc, ALU.mult)
            p.dma("gpsimd", yscr[seq, i * 128:(i + 1) * 128,
                                 hg * 512:(hg + 1) * 512], yt)

    if cut == "chunks":
        for i in range(NT):
            xr = com.xrot[0]
            p.dma("sync", xr[:, :], x_h[seq, i * 128:(i + 1) * 128, :])
            s_f = com.scr[0]
            p.copy(s_f[:, :], xr[:, :])
            p.dma("sync", y_h[seq, i * 128:(i + 1) * 128, :], s_f[:, :])
        return
    for i in range(NT):
        p.dma("sync", AR[i // 2][:, (i % 2) * 1024:(i % 2 + 1) * 1024],
              yscr[seq, i * 128:(i + 1) * 128, :])
    for j in range(8):
        for i in range(NT):
            ps = psT[i % 2]
            p.transpose(ps[:, :], AR[i // 2][:, (i % 2) * 1024 + j * 128:
                                             (i % 2) * 1024 + (j + 1) * 128],
                        com.ident[:, :])
            p.copy(AR[8 + j][:, i * 128:(i + 1) * 128], ps[:, :],
                   eng="vector" if i % 2 == 0 else "scalar")
    def wot_ap(kc):
        if kc < 4:
            return scrb[kc // 2][:, (kc % 2) * 1024:(kc % 2 + 1) * 1024]
        if kc < 6:
            return bc2b[:, (kc - 4) * 1024:(kc - 3) * 1024]
        return bc3b[:, (kc - 6) * 1024:(kc - 5) * 1024]

    for kc in range(NC):
        for m in range(NC):
            wcl = com.wcol[m % 2]
            p.dma("gpsimd", wcl[:, :],
                  io["L0_attn_Wo"][m * 128:(m + 1) * 128, kc * 128:(kc + 1) * 128])
            ps = psT[m % 2]
            p.transpose(ps[:, :], wcl[:, :], com.ident[:, :])
            p.copy(wot_ap(kc)[:, m * 128:(m + 1) * 128], ps[:, :],
                   eng="vector" if m % 2 == 0 else "scalar")
    for i in range(NT):
        for dh in range(2):
            ps = psO[dh]
            for kc in range(NC):
                p.mm(ps[:, :], AR[8 + kc][:, i * 128:(i + 1) * 128],
                     wot_ap(kc)[:, dh * 512:(dh + 1) * 512],
                     start=(kc == 0), stop=(kc == NC - 1))
        s_f = com.bc1
        for dh in range(2):
            p.copy(s_f[:, dh * 512:(dh + 1) * 512], psO[dh][:, :])
        emit_final_update(p, com, x_h, y_h, seq, i, s_f[:, :],
                          gbuf=XN[5][:, :].bitcast(F32))


def emit_layer0(p, com, io, seq):
    import os
    parts = os.environ.get("L0_PARTS", "all")
    emit_ln_stage(p, com, io["x"], seq)
    if parts in ("all", "ffn"):
        emit_l0_ffn(p, com, io, seq)
    else:
        for i in range(NT):
            p.memset(com.GA[i][:, :], 0.0)
    if parts in ("all", "attn"):
        emit_l0_attn(p, com, io, seq, io["x"], io["y"])
    else:
        for i in range(NT):
            xr = com.xrot[0]
            p.dma("sync", xr[:, :], io["x"][seq, i * 128:(i + 1) * 128, :])
            s_f = com.scr[0]
            p.act(s_f[:, :], com.GA[i][:, :], AF.Silu)
            p.tt(s_f[:, :], s_f[:, :], xr[:, :], ALU.add)
            p.dma("sync", io["y"][seq, i * 128:(i + 1) * 128, :], s_f[:, :])


L0_INPUT_NAMES = ["L0_attn_Wq", "L0_attn_Wk", "L0_attn_Wv", "L0_attn_Wo",
                  "L0_attn_g_weight", "L0_ffn_router_DE",
                  "L0_ffn_w_in_shared_FD", "L0_ffn_w_out_shared_DF",
                  "L0_ffn_w_swiglu_FD", "L0_ffn_w_in_eDF",
                  "L0_ffn_w_swiglu_eDF", "L0_ffn_w_out_eFD"]

L0_SHAPES = {"L0_attn_Wq": (256, C), "L0_attn_Wk": (256, C),
             "L0_attn_Wv": (C, C), "L0_attn_Wo": (C, C),
             "L0_attn_g_weight": (64,), "L0_ffn_router_DE": (C, 8),
             "L0_ffn_w_in_shared_FD": (2048, C),
             "L0_ffn_w_out_shared_DF": (C, 2048),
             "L0_ffn_w_swiglu_FD": (2048, C),
             "L0_ffn_w_in_eDF": (8, C, 2048),
             "L0_ffn_w_swiglu_eDF": (8, C, 2048),
             "L0_ffn_w_out_eFD": (8, 2048, C)}


def build_layer0_program():
    global _DECL_CACHE
    _DECL_CACHE = {}
    nc = bass.Bass("TRN2", target_bir_lowering=False, debug=False)
    p = Prog(nc)
    com = Com(p)
    io = {"x": nc.dram_tensor("x", [NSEQ, T, C], F32, kind="ExternalInput").ap(),
          "y": declare_out(nc, "y", (NSEQ, T, C)),
          "l0y": nc.dram_tensor("l0y", [NSEQ, T, C], BF16, kind="Internal").ap(),
          "l0qk": nc.dram_tensor("l0qk", [NSEQ, 2, T, 256], BF16, kind="Internal").ap()}
    for n in L0_INPUT_NAMES:
        io[n] = declare_in(nc, n, L0_SHAPES[n])
    for seq in range(NSEQ):
        emit_layer0(p, com, io, seq)
    p.build()
    return nc, p


def run_layer0(x, weights):
    nc, p = build_layer0_program()
    cst = make_constants()
    in_maps = []
    for core in range(8):
        m = {"x": np.ascontiguousarray(x[core * 2:(core + 1) * 2])}
        for k, v in weights.items():
            m[k] = np.asarray(v, dtype=np.float32)
        m.update(cst)
        in_maps.append(m)
    res = run_bass_kernel_spmd(nc, in_maps, core_ids=list(range(8)))
    out = np.concatenate([r["y"] for r in res.results], axis=0)
    p.stack.close()
    return out
